# version 05
# baseline (speedup 1.0000x reference)
import numpy as np
import ml_dtypes
from collections import defaultdict
from contextlib import ExitStack

import concourse.bass as bass
import concourse.mybir as mybir
from concourse.bass_utils import run_bass_kernel_spmd

F32 = mybir.dt.float32
BF16 = mybir.dt.bfloat16
AF = mybir.ActivationFunctionType
ALU = mybir.AluOpType
AX = mybir.AxisListType

P = 128
D = 1024
KD = D // P
NCORE = 8
SPAN = 4096
NG = SPAN // 512
B_, T_ = 2, 16384
H, DK, DV = 16, 64, 64
NE, FF = 8, 4096
KF = FF // P
LN_EPS = 1e-5
MOE_EPS = 1e-6
RMS_EPS = 1e-6
FM, FI = 64, 16

ENGS = ("sync", "tensor", "vector", "scalar", "gpsimd")


class LB:

    def __init__(self, nc, ctx):
        self.nc = nc
        self.ctx = ctx
        self.q = {e: [] for e in ENGS}
        self.sems = {}
        self.cnt = defaultdict(int)

    def sem(self, name):
        if name not in self.sems:
            self.sems[name] = self.ctx.enter_context(self.nc.semaphore(name))
        return self.sems[name]

    def op(self, eng, fn, inc=None, dinc=None, dma=False):
        if inc is None:
            inc = dinc
        drain = (eng in ("vector", "scalar", "gpsimd")) and not dma
        if inc is not None:
            s = self.sem(inc)
            self.cnt[inc] += 16 if dma else 1
        else:
            s = None
        d = 16 if dma else 1

        def run(e, fn=fn, s=s, d=d, drain=drain):
            instr = fn(e)
            if drain:
                e.drain()
                if s is not None:
                    e.nop().then_inc(s, d)
            elif s is not None:
                instr.then_inc(s, d)
        self.q[eng].append(run)

    def wait(self, eng, sname, val):
        if val <= 0:
            return
        s = self.sem(sname)
        self.q[eng].append(lambda e, s=s, v=val: e.wait_ge(s, v))

    def finish(self):
        block = self.ctx.enter_context(self.nc.Block())
        for eng in ENGS:
            ops = self.q[eng]
            if not ops:
                continue
            def section(e, ops=ops):
                for f in ops:
                    f(e)
            getattr(block, eng)(section)


def make_ident(lb, ident_bf):
    def f(g):
        g.memset(ident_bf[:], 1.0)
        g.drain()
        return g.affine_select(out=ident_bf[:], in_=ident_bf[:],
                               compare_op=ALU.is_equal, fill=0.0, base=0,
                               pattern=[[-1, P]], channel_multiplier=1)
    lb.op("gpsimd", f, dinc="CONST")


def make_mask(lb, mask, strict):
    cmp = ALU.is_gt if strict else ALU.is_ge
    def f(g):
        g.memset(mask[:], 1.0)
        g.drain()
        return g.affine_select(out=mask[:], in_=mask[:], compare_op=cmp,
                               fill=0.0, base=0, pattern=[[1, P]],
                               channel_multiplier=-1)
    lb.op("gpsimd", f, dinc="CONST")


def const_col(lb, nc_sbuf_col, value):
    lb.op("gpsimd", lambda g: g.memset(nc_sbuf_col[:], value), dinc="CONST")


def emit_ln(lb, xin, xout, wk, eps_col, nfree, out_raw=None):
    sq, c0, c1, c2, c3, c4 = wk["sq"], wk["c0"], wk["c1"], wk["c2"], wk["c3"], wk["c4"]
    lb.op("vector", lambda v: v.tensor_mul(sq[:, 0:nfree], xin, xin))
    lb.op("vector", lambda v: v.reduce_sum(c0[:], xin, AX.X))
    lb.op("vector", lambda v: v.reduce_sum(c1[:], sq[:, 0:nfree], AX.X))
    lb.op("vector", lambda v: v.tensor_scalar(c0[:], c0[:], 1.0 / nfree, None, ALU.mult))
    lb.op("vector", lambda v: v.tensor_scalar(c1[:], c1[:], 1.0 / nfree, None, ALU.mult))
    lb.op("vector", lambda v: v.tensor_mul(c2[:], c0[:], c0[:]))
    lb.op("vector", lambda v: v.tensor_sub(c1[:], c1[:], c2[:]), dinc="LNV")
    lb.wait("scalar", "LNV", lb.cnt["LNV"])
    lb.op("scalar", lambda s: s.activation(c2[:], c1[:], AF.Sqrt, bias=eps_col), dinc="LNS")
    lb.wait("vector", "LNS", lb.cnt["LNS"])
    lb.op("vector", lambda v: v.reciprocal(c3[:], c2[:]))
    lb.op("vector", lambda v: v.tensor_scalar(c4[:], c0[:], c3[:], -1.0, ALU.mult, ALU.mult))
    lb.op("vector", lambda v: v.tensor_scalar(xout, xin, c3[:], c4[:], ALU.mult, ALU.add),
          dinc="LNV")
    if out_raw is not None:
        lb.op("vector", lambda v: v.tensor_scalar(out_raw, xin, c3[:], c4[:], ALU.mult, ALU.add),
              dinc="LNV")


def emit_transposes(lb, get_src, xnT_dst, ident_bf, pst2, nt):
    items = [(k, tt) for k in range(KD) for tt in range(nt)]
    assert len(items) % 4 == 0
    nbatch = len(items) // 4
    for bi in range(nbatch):
        pp = bi % 2
        lb.wait("tensor", "TRE", lb.cnt["TRE"] - 1)
        for j in range(4):
            k, tt = items[bi * 4 + j]
            lb.op("tensor",
                  lambda t, k=k, tt=tt, pp=pp, j=j:
                  t.transpose(pst2[pp][:, j * P:(j + 1) * P], get_src(k, tt),
                              ident_bf[:]),
                  inc="TRP" if j == 3 else None)
        lb.wait("vector", "TRP", lb.cnt["TRP"])
        for j in range(4):
            k, tt = items[bi * 4 + j]
            lb.op("vector",
                  lambda v, k=k, tt=tt, pp=pp, j=j:
                  v.tensor_copy(xnT_dst(k, tt), pst2[pp][:, j * P:(j + 1) * P]),
                  dinc="TRE" if j == 3 else None)


def stage_weights(lb, pairs, stg, ld_sem="WLD", cast_sem="WCST"):
    for i, (src, dst) in enumerate(pairs):
        n = src.shape[-1]
        pp = src.shape[-2]
        if i > 0:
            lb.wait("sync", cast_sem, i)
        lb.op("sync", lambda s, src=src, n=n, pp=pp: s.dma_start(stg[0:pp, 0:n], src),
              inc=ld_sem, dma=True)
        lb.wait("gpsimd", ld_sem, lb.cnt[ld_sem])
        lb.op("gpsimd", lambda g, dst=dst, n=n, pp=pp: g.tensor_copy(dst, stg[0:pp, 0:n]),
              dinc=cast_sem)



def build_launch_a():
    nc = bass.Bass("TRN2", target_bir_lowering=False)
    x = nc.dram_tensor("x", [SPAN, D], F32, kind="ExternalInput").ap()
    Wq = nc.dram_tensor("Wq", [D, D], F32, kind="ExternalInput").ap()
    Wk = nc.dram_tensor("Wk", [D, D], F32, kind="ExternalInput").ap()
    Wv = nc.dram_tensor("Wv", [D, D], F32, kind="ExternalInput").ap()
    qT = nc.dram_tensor("qT", [D, SPAN], BF16, kind="ExternalOutput").ap()
    kT = nc.dram_tensor("kT", [D, SPAN], BF16, kind="ExternalOutput").ap()
    qm = nc.dram_tensor("qm", [SPAN, D], BF16, kind="ExternalOutput").ap()
    km = nc.dram_tensor("km", [SPAN, D], BF16, kind="ExternalOutput").ap()
    vm = nc.dram_tensor("vm", [SPAN, D], BF16, kind="ExternalOutput").ap()

    with ExitStack() as ctx:
        E = ctx.enter_context
        lb = LB(nc, ctx)
        wqb = E(nc.sbuf_tensor("wqb", [P, KD * D], BF16))
        wkb = E(nc.sbuf_tensor("wkb", [P, KD * D], BF16))
        wvb = E(nc.sbuf_tensor("wvb", [P, KD * D], BF16))
        wstg = E(nc.sbuf_tensor("wstg", [P, D], F32))
        xs = [E(nc.sbuf_tensor(f"xs{t}", [P, D], F32)) for t in range(4)]
        xng = [E(nc.sbuf_tensor(f"xng{t}", [P, D], BF16)) for t in range(4)]
        xnT = [E(nc.sbuf_tensor(f"xnT{k}", [P, 512], BF16)) for k in range(KD)]
        ident = E(nc.sbuf_tensor("ident", [P, P], BF16))
        sq = E(nc.sbuf_tensor("sq", [P, D], F32))
        cols = [E(nc.sbuf_tensor(f"col{i}", [P, 1], F32)) for i in range(5)]
        epsc = E(nc.sbuf_tensor("epsc", [P, 1], F32))
        ev = [E(nc.sbuf_tensor(f"ev{i}", [P, 512], BF16)) for i in range(4)]
        pst2 = [E(nc.psum_tensor(f"pst{i}", [P, 4 * P], BF16)) for i in range(2)]
        psp = [E(nc.psum_tensor(f"psp{i}", [P, 512], F32)) for i in range(4)]
        wk = {"sq": sq, "c0": cols[0], "c1": cols[1], "c2": cols[2],
              "c3": cols[3], "c4": cols[4]}

        make_ident(lb, ident)
        const_col(lb, epsc, LN_EPS)
        lb.wait("scalar", "CONST", lb.cnt["CONST"])

        wpairs = []
        for k in range(KD):
            wpairs.append((Wq[k * P:(k + 1) * P, :], wqb[:, k * D:(k + 1) * D]))
            wpairs.append((Wk[k * P:(k + 1) * P, :], wkb[:, k * D:(k + 1) * D]))
            wpairs.append((Wv[k * P:(k + 1) * P, :], wvb[:, k * D:(k + 1) * D]))
        stage_weights(lb, wpairs, wstg)
        n_wcast = lb.cnt["WCST"]

        nmm = 0
        for g in range(NG):
            if g > 0:
                lb.wait("sync", "LNV", lnv_after)
            for tt in range(4):
                lb.op("sync", lambda s, g=g, tt=tt:
                      s.dma_start(xs[tt][:], x[g * 512 + tt * P:g * 512 + (tt + 1) * P, :]),
                      inc="XLD", dma=True)
            lb.wait("vector", "XLD", lb.cnt["XLD"])
            lb.wait("vector", "TRP", 8 * g)
            for tt in range(4):
                emit_ln(lb, xs[tt][:], xng[tt][:], wk, epsc[:], D)
            lnv_after = lb.cnt["LNV"]
            if g == 0:
                lb.wait("tensor", "WCST", n_wcast)
                lb.wait("tensor", "CONST", lb.cnt["CONST"])
            lb.wait("tensor", "LNV", lnv_after)
            emit_transposes(lb, lambda k, tt: xng[tt][:, k * P:(k + 1) * P],
                            lambda k, tt: xnT[k][:, tt * P:(tt + 1) * P],
                            ident, pst2, 4)
            teready = lb.cnt["TRE"]
            lb.wait("tensor", "TRE", teready)

            def proj(mk_mm, dst_dma):
                nonlocal nmm
                bank = nmm % 4
                if nmm >= 4:
                    lb.wait("tensor", "PEV", nmm - 3)
                for k in range(KD):
                    lb.op("tensor", mk_mm(k, bank),
                          inc="PMM" if k == KD - 1 else None)
                lb.wait("scalar", "PMM", lb.cnt["PMM"])
                if nmm >= 4:
                    lb.wait("scalar", f"ST{bank}", lb.cnt[f"ST{bank}"])
                lb.op("scalar", lambda s, bank=bank: s.copy(ev[bank][:], psp[bank][:]),
                      dinc="PEV")
                lb.wait("sync", "PEV", lb.cnt["PEV"])
                lb.op("sync", dst_dma(bank), inc=f"ST{bank}", dma=True)
                nmm += 1

            for (wb, dstT) in ((wqb, qT), (wkb, kT)):
                for hd in range(KD):
                    proj(lambda k, bank, wb=wb, hd=hd:
                         (lambda t: t.matmul(psp[bank][:],
                                             wb[:, k * D + hd * P:k * D + (hd + 1) * P],
                                             xnT[k][:], start=(k == 0), stop=(k == KD - 1))),
                         lambda bank, dstT=dstT, hd=hd, g=g:
                         (lambda s: s.dma_start(
                             dstT[hd * P:(hd + 1) * P, g * 512:(g + 1) * 512], ev[bank][:])))
            for (wb, dstm) in ((wqb, qm), (wkb, km), (wvb, vm)):
                for tt in range(4):
                    for db in range(2):
                        proj(lambda k, bank, wb=wb, tt=tt, db=db:
                             (lambda t: t.matmul(psp[bank][:],
                                                 xnT[k][:, tt * P:(tt + 1) * P],
                                                 wb[:, k * D + db * 512:k * D + (db + 1) * 512],
                                                 start=(k == 0), stop=(k == KD - 1))),
                             lambda bank, dstm=dstm, g=g, tt=tt, db=db:
                             (lambda s: s.dma_start(
                                 dstm[g * 512 + tt * P:g * 512 + (tt + 1) * P,
                                      db * 512:(db + 1) * 512], ev[bank][:])))
        for bank in range(4):
            lb.wait("sync", f"ST{bank}", lb.cnt[f"ST{bank}"])
        lb.finish()
    return nc



def build_launch_b(T=T_):
    NH = 4
    NCH = T // P
    nc = bass.Bass("TRN2", target_bir_lowering=False)
    qTh = nc.dram_tensor("qTh", [NH, DK, T], BF16, kind="ExternalInput").ap()
    kTh = nc.dram_tensor("kTh", [NH, DK, T], BF16, kind="ExternalInput").ap()
    qmh = nc.dram_tensor("qmh", [NH, T, DK], BF16, kind="ExternalInput").ap()
    kmh = nc.dram_tensor("kmh", [NH, T, DK], BF16, kind="ExternalInput").ap()
    vmh = nc.dram_tensor("vmh", [NH, T, DV], BF16, kind="ExternalInput").ap()
    oTh = nc.dram_tensor("oTh", [NH, DV, T], BF16, kind="ExternalOutput").ap()

    with ExitStack() as ctx:
        E = ctx.enter_context
        lb = LB(nc, ctx)
        qTc = [[E(nc.sbuf_tensor(f"qTc{h}_{p}", [DK, P], BF16)) for p in range(2)]
               for h in range(NH)]
        kTc = [[E(nc.sbuf_tensor(f"kTc{h}_{p}", [DK, P], BF16)) for p in range(2)]
               for h in range(NH)]
        qmc = [[E(nc.sbuf_tensor(f"qmc{h}_{p}", [P, DK], BF16)) for p in range(2)]
               for h in range(NH)]
        kmc = [[E(nc.sbuf_tensor(f"kmc{h}_{p}", [P, DK], BF16)) for p in range(2)]
               for h in range(NH)]
        vmc = [[E(nc.sbuf_tensor(f"vmc{h}_{p}", [P, DV], BF16)) for p in range(2)]
               for h in range(NH)]
        oTp = [E(nc.sbuf_tensor(f"oTp{i}", [P, T], BF16)) for i in range(2)]
        Sf = [E(nc.sbuf_tensor(f"Sf{h}", [DK, DK], F32)) for h in range(NH)]
        Cf = [E(nc.sbuf_tensor(f"Cf{h}", [DK, DV], F32)) for h in range(NH)]
        Gf = [E(nc.sbuf_tensor(f"Gf{h}", [DK, DV], F32)) for h in range(NH)]
        Sb = [E(nc.sbuf_tensor(f"Sb{h}", [DK, DK], BF16)) for h in range(NH)]
        bBtp = [E(nc.sbuf_tensor(f"bBtp{h}", [P, P], BF16)) for h in range(NH)]
        bBtm = [E(nc.sbuf_tensor(f"bBtm{h}", [P, P], BF16)) for h in range(NH)]
        bBqm = [E(nc.sbuf_tensor(f"bBqm{h}", [P, P], BF16)) for h in range(NH)]
        bPT = [E(nc.sbuf_tensor(f"bPT{h}", [DK, P], BF16)) for h in range(NH)]
        bZb = [E(nc.sbuf_tensor(f"bZb{h}", [DK, DV], BF16)) for h in range(NH)]
        bMt = [E(nc.sbuf_tensor(f"bMt{h}", [P, P], BF16)) for h in range(NH)]
        bKKf = [E(nc.sbuf_tensor(f"bKKf{h}", [DK, DK], F32)) for h in range(NH)]
        bInner = [E(nc.sbuf_tensor(f"bInner{h}", [P, DV], BF16)) for h in range(NH)]
        m_ge = E(nc.sbuf_tensor("m_ge", [P, P], F32))
        m_gt = E(nc.sbuf_tensor("m_gt", [P, P], F32))
        pb1 = [E(nc.psum_tensor(f"pb1_{h}", [P, 512], F32)) for h in range(NH)]
        pb2 = [E(nc.psum_tensor(f"pb2_{h}", [P, 512], F32)) for h in range(NH)]

        make_mask(lb, m_ge, strict=False)
        make_mask(lb, m_gt, strict=True)
        lb.wait("vector", "CONST", lb.cnt["CONST"])
        vb9 = [0] * NH
        sb10 = [0] * NH
        for h in range(NH):
            lb.op("vector", lambda v, h=h: v.memzero(Sf[h][:]))
            lb.op("vector", lambda v, h=h: v.memzero(Cf[h][:]))
            lb.op("vector", lambda v, h=h: v.memzero(Gf[h][:]), dinc="VB")
            vb9[h] = lb.cnt["VB"]
            lb.op("scalar", lambda s, h=h: s.memzero(Sb[h][:]), dinc="SB2")
            sb10[h] = lb.cnt["SB2"]

        def emit_loads(ch, h):
            p = ch % 2
            sem = f"LD{h}_{p}"
            lb.op("sync", lambda s, ch=ch, h=h, p=p:
                  s.dma_start(qTc[h][p][:], qTh[h, :, ch * P:(ch + 1) * P]),
                  inc=sem, dma=True)
            lb.op("sync", lambda s, ch=ch, h=h, p=p:
                  s.dma_start(kTc[h][p][:], kTh[h, :, ch * P:(ch + 1) * P]),
                  inc=sem, dma=True)
            lb.op("scalar", lambda s, ch=ch, h=h, p=p:
                  s.dma_start(qmc[h][p][:], qmh[h, ch * P:(ch + 1) * P, :]),
                  inc=sem, dma=True)
            lb.op("scalar", lambda s, ch=ch, h=h, p=p:
                  s.dma_start(kmc[h][p][:], kmh[h, ch * P:(ch + 1) * P, :]),
                  inc=sem, dma=True)
            lb.op("sync", lambda s, ch=ch, h=h, p=p:
                  s.dma_start(vmc[h][p][:], vmh[h, ch * P:(ch + 1) * P, :]),
                  inc=sem, dma=True)

        for h in range(NH):
            emit_loads(0, h)
            emit_loads(1, h)

        tb4 = [0] * NH
        vb5 = [0] * NH
        tb6 = [0] * NH
        vb7 = [0] * NH
        tb8 = [0] * NH
        regions = []
        for h in range(NH):
            regions.append({
                "A": pb1[h][:, 0:P],
                "B": pb1[h][:, P:2 * P],
                "PT": pb1[h][0:DK, 2 * P:3 * P],
                "Z": pb1[h][0:DK, 3 * P:3 * P + DV],
                "I": pb1[h][:, 3 * P + DV:3 * P + 2 * DV],
                "M": pb2[h][:, 0:P],
                "O": pb2[h][0:DV, P:2 * P],
                "S": pb2[h][0:DK, 2 * P:2 * P + 3 * DK],
                "G": pb2[h][0:DK, 2 * P + 3 * DK:2 * P + 4 * DK],
            })

        for ch in range(NCH):
            p = ch % 2
            for h in range(NH):
                r = regions[h]
                QT = qTc[h][p][:]
                KT = kTc[h][p][:]
                lb.wait("tensor", f"LD{h}_{p}", lb.cnt[f"LD{h}_{p}"])
                lb.wait("tensor", "VB", vb9[h])
                lb.wait("tensor", "SB2", sb10[h])
                lb.op("tensor", lambda t, KT=KT, QT=QT, r=r: t.matmul(r["A"], KT, QT))
                lb.op("tensor", lambda t, KT=KT, QT=QT, r=r: t.matmul(r["B"], QT, KT))
                lb.op("tensor", lambda t, QT=QT, h=h, r=r: t.matmul(r["PT"], Sb[h][:], QT))
                lb.op("tensor", lambda t, h=h, r=r:
                      t.matmul(r["Z"], Sf[h][:], Cf[h][:]), inc="TB")
                tb4[h] = lb.cnt["TB"]
            for h in range(NH):
                r = regions[h]
                lb.wait("vector", "TB", tb4[h])
                lb.op("vector", lambda v, h=h, r=r: v.tensor_copy(bBtp[h][:], r["A"]))
                lb.op("vector", lambda v, h=h, r=r:
                      v.tensor_mul(bBtm[h][:], r["A"], m_ge[:]))
                lb.op("vector", lambda v, h=h, r=r:
                      v.tensor_mul(bBqm[h][:], r["B"], m_gt[:]))
                lb.op("vector", lambda v, h=h, r=r: v.tensor_copy(bPT[h][:], r["PT"]))
                lb.op("vector", lambda v, h=h, r=r:
                      v.tensor_sub(bZb[h][:], r["Z"], Gf[h][:]), dinc="VB")
                vb5[h] = lb.cnt["VB"]
            for h in range(NH):
                r = regions[h]
                QT = qTc[h][p][:]
                Km = kmc[h][p][:]
                Qm = qmc[h][p][:]
                Vm = vmc[h][p][:]
                lb.wait("tensor", "VB", vb5[h])
                lb.op("tensor", lambda t, h=h, r=r:
                      t.matmul(r["M"], bBtm[h][:], bBtp[h][:], start=True, stop=False))
                lb.op("tensor", lambda t, h=h, QT=QT, r=r:
                      t.matmul(r["M"], QT, bPT[h][:], start=False, stop=True))
                lb.op("tensor", lambda t, h=h, Vm=Vm, r=r:
                      t.matmul(r["I"], bBqm[h][:], Vm))
                lb.op("tensor", lambda t, Km=Km, r=r:
                      t.matmul(r["S"][:, 0:DK], Km, Km))
                lb.op("tensor", lambda t, Qm=Qm, Vm=Vm, r=r:
                      t.matmul(r["S"][:, DK:2 * DK], Qm, Vm), inc="TB")
                tb6[h] = lb.cnt["TB"]
            for h in range(NH):
                r = regions[h]
                lb.wait("vector", "TB", tb6[h])
                lb.op("vector", lambda v, h=h, r=r:
                      v.tensor_mul(bMt[h][:], r["M"], m_ge[:]))
                lb.op("vector", lambda v, h=h, r=r:
                      v.tensor_copy(bKKf[h][:], r["S"][:, 0:DK]))
                lb.op("vector", lambda v, h=h, r=r:
                      v.tensor_copy(bInner[h][:], r["I"]), dinc="VB")
                vb7[h] = lb.cnt["VB"]
            for h in range(NH):
                r = regions[h]
                QT = qTc[h][p][:]
                Km = kmc[h][p][:]
                lb.wait("tensor", "VB", vb7[h])
                Vm8 = vmc[h][p][:]
                lb.op("tensor", lambda t, h=h, r=r, Vm8=Vm8:
                      t.matmul(r["O"], Vm8, bMt[h][:], start=True, stop=False))
                lb.op("tensor", lambda t, h=h, QT=QT, r=r:
                      t.matmul(r["O"], bZb[h][:], QT, start=False, stop=True))
                lb.op("tensor", lambda t, h=h, Km=Km, r=r:
                      t.matmul(r["S"][:, 2 * DK:3 * DK], Km, bInner[h][:]))
                lb.op("tensor", lambda t, h=h, r=r:
                      t.matmul(r["G"], bKKf[h][:], Cf[h][:]), inc="TB")
                tb8[h] = lb.cnt["TB"]
                if ch + 2 < NCH:
                    lb.wait("sync", "TB", tb8[h])
                    lb.wait("scalar", "TB", tb8[h])
                    emit_loads(ch + 2, h)
            for h in range(NH):
                r = regions[h]
                lb.wait("vector", "TB", tb8[h])
                lb.op("vector", lambda v, h=h, ch=ch, r=r:
                      v.tensor_copy(oTp[h // 2][64 * (h % 2):64 * (h % 2) + DV,
                                    ch * P:(ch + 1) * P], r["O"]))
                lb.op("vector", lambda v, h=h, r=r:
                      v.tensor_add(Gf[h][:], Gf[h][:], r["S"][:, 2 * DK:3 * DK]))
                lb.op("vector", lambda v, h=h, r=r:
                      v.tensor_add(Gf[h][:], Gf[h][:], r["G"]))
                lb.op("vector", lambda v, h=h, r=r:
                      v.tensor_add(Cf[h][:], Cf[h][:], r["S"][:, DK:2 * DK]))
                lb.op("vector", lambda v, h=h:
                      v.tensor_add(Sf[h][:], Sf[h][:], bKKf[h][:]), dinc="VB")
                vb9[h] = lb.cnt["VB"]
                if ch < NCH - 1:
                    lb.wait("scalar", "VB", vb9[h])
                    lb.op("scalar", lambda s, h=h: s.copy(Sb[h][:], Sf[h][:]),
                          dinc="SB2")
                    sb10[h] = lb.cnt["SB2"]
        lb.wait("sync", "VB", lb.cnt["VB"])
        for h in range(NH):
            lb.op("sync", lambda s, h=h:
                  s.dma_start(oTh[h], oTp[h // 2][64 * (h % 2):64 * (h % 2) + DV, :]),
                  inc="OST", dma=True)
        lb.wait("sync", "OST", lb.cnt["OST"])
        lb.finish()
    return nc



def build_launch_w():
    nc = bass.Bass("TRN2", target_bir_lowering=False)
    w1 = nc.dram_tensor("w1", [D, FF], F32, kind="ExternalInput").ap()
    w2 = nc.dram_tensor("w2", [FF, D], F32, kind="ExternalInput").ap()
    w1o = nc.dram_tensor("w1o", [D, FF], BF16, kind="ExternalOutput").ap()
    w2o = nc.dram_tensor("w2o", [FF, D], BF16, kind="ExternalOutput").ap()

    tiles = [(w1[k * P:(k + 1) * P, :], w1o[k * P:(k + 1) * P, :], FF)
             for k in range(KD)]
    tiles += [(w2[f * P:(f + 1) * P, :], w2o[f * P:(f + 1) * P, :], D)
              for f in range(KF)]

    with ExitStack() as ctx:
        E = ctx.enter_context
        lb = LB(nc, ctx)
        nbuf = 4
        stg = [E(nc.sbuf_tensor(f"stg{i}", [P, FF], F32)) for i in range(nbuf)]
        out = [E(nc.sbuf_tensor(f"out{i}", [P, FF], BF16)) for i in range(nbuf)]
        for i, (srcp, dst, n) in enumerate(tiles):
            b = i % nbuf
            ld, cst, st = f"LD{b}", f"CS{b}", f"ST{b}"
            lb.wait("sync", st, lb.cnt[st])
            lb.op("sync", lambda s, srcp=srcp, b=b, n=n:
                  s.dma_start(stg[b][:, 0:n], srcp), inc=ld, dma=True)
            lb.wait("gpsimd", ld, lb.cnt[ld])
            lb.op("gpsimd", lambda g, b=b, n=n:
                  g.tensor_copy(out[b][:, 0:n], stg[b][:, 0:n]), dinc=cst)
            lb.wait("sync", cst, lb.cnt[cst])
            lb.op("sync", lambda s, dst=dst, b=b, n=n:
                  s.dma_start(dst, out[b][:, 0:n]), inc=st, dma=True)
        for b in range(nbuf):
            lb.wait("sync", f"ST{b}", lb.cnt[f"ST{b}"])
        lb.finish()
    return nc



def build_launch_c(ng=NG, ne=NE):
    span = ng * 512
    nc = bass.Bass("TRN2", target_bir_lowering=False)
    x = nc.dram_tensor("x", [span, D], F32, kind="ExternalInput").ap()
    oT = nc.dram_tensor("oT", [D, span], BF16, kind="ExternalInput").ap()
    Wo = nc.dram_tensor("Wo", [D, D], F32, kind="ExternalInput").ap()
    rw = nc.dram_tensor("rw", [D, NE], F32, kind="ExternalInput").ap()
    w1b = nc.dram_tensor("w1b", [NE, D, FF], BF16, kind="ExternalInput").ap()
    b1L = nc.dram_tensor("b1L", [NE, P, KF], F32, kind="ExternalInput").ap()
    w2b = nc.dram_tensor("w2b", [NE, FF, D], BF16, kind="ExternalInput").ap()
    b2r = nc.dram_tensor("b2r", [NE, 1, D], F32, kind="ExternalInput").ap()
    gmr = nc.dram_tensor("gmr", [NE, 1, D], F32, kind="ExternalInput").ap()
    btr = nc.dram_tensor("btr", [NE, 1, D], F32, kind="ExternalInput").ap()
    x2 = nc.dram_tensor("x2", [span, D], F32, kind="ExternalOutput").ap()
    xn2T = nc.dram_tensor("xn2T", [D, span], BF16, kind="ExternalOutput").ap()
    x1d = nc.dram_tensor("x1d", [span, D], F32, kind="Internal").ap()
    xn1fd = nc.dram_tensor("xn1fd", [span, D], F32, kind="Internal").ap()
    xn1Td = nc.dram_tensor("xn1Td", [D, span], BF16, kind="Internal").ap()
    yacc = nc.dram_tensor("yacc", [span, D], F32, kind="Internal").ap()

    with ExitStack() as ctx:
        E = ctx.enter_context
        lb = LB(nc, ctx)
        WB1 = E(nc.sbuf_tensor("WB1", [P, KD * FF], BF16))
        WB2 = E(nc.sbuf_tensor("WB2", [P, KF * D], BF16))
        HT = E(nc.sbuf_tensor("HT", [P, KF * 512], BF16))
        XT = E(nc.sbuf_tensor("XT", [P, KD * 512], BF16))
        XN = E(nc.sbuf_tensor("XN", [P, 4 * D], BF16))
        X1T = E(nc.sbuf_tensor("X1T", [P, D], F32))
        RES = E(nc.sbuf_tensor("RES", [P, D], F32))
        NRM = E(nc.sbuf_tensor("NRM", [P, D], F32))
        BC = E(nc.sbuf_tensor("BC", [P, 2 * D], F32))
        B1C = E(nc.sbuf_tensor("B1C", [P, KF], F32))
        GATES = E(nc.sbuf_tensor("GATES", [P, ng * 4 * NE], F32))
        SQ = E(nc.sbuf_tensor("SQ", [P, D], F32))
        RWF = E(nc.sbuf_tensor("RWF", [P, KD * NE], F32))
        RWHI = E(nc.sbuf_tensor("RWHI", [P, KD * NE], BF16))
        RWLO = E(nc.sbuf_tensor("RWLO", [P, KD * NE], BF16))
        IOT = E(nc.sbuf_tensor("IOT", [P, NE], F32))
        T8 = [E(nc.sbuf_tensor(f"T8_{i}", [P, NE], F32)) for i in range(3)]
        ident = E(nc.sbuf_tensor("ident", [P, P], BF16))
        ones1 = E(nc.sbuf_tensor("ones1", [1, P], F32))
        ones1b = E(nc.sbuf_tensor("ones1b", [1, P], BF16))
        row1 = XN[:].bitcast(F32)[0:1, 0:D]
        row1b = XN[:][0:1, 2 * D:3 * D]
        cols = [E(nc.sbuf_tensor(f"col{i}", [P, 1], F32)) for i in range(9)]
        epsc = E(nc.sbuf_tensor("epsc", [P, 1], F32))
        epsm = E(nc.sbuf_tensor("epsm", [P, 1], F32))
        STG = HT[:].bitcast(F32)[:, 0:D]
        LOG = HT[:][:, 2 * D:6 * D]
        LOGT = HT[:][:, 6 * D:10 * D]
        ps_l1 = [E(nc.psum_tensor(f"ps_l1{i}", [P, 512], F32)) for i in range(2)]
        ps_l2 = [E(nc.psum_tensor(f"ps_l2{i}", [P, 512], F32)) for i in range(4)]
        ps_tr = [E(nc.psum_tensor(f"ps_tr{i}", [P, 4 * P], BF16)) for i in range(2)]
        ps_bc = ps_l1[0]
        wk = {"sq": SQ, "c0": cols[0], "c1": cols[1], "c2": cols[2],
              "c3": cols[3], "c4": cols[4]}

        make_ident(lb, ident)
        const_col(lb, epsc, LN_EPS)
        const_col(lb, epsm, MOE_EPS)
        lb.op("gpsimd", lambda g: g.memset(ones1[:], 1.0), dinc="CONST")
        lb.op("gpsimd", lambda g: g.memset(ones1b[:], 1.0), dinc="CONST")
        lb.op("gpsimd", lambda g: g.iota(IOT[:], pattern=[[1, NE]], base=0,
                                         channel_multiplier=0,
                                         allow_small_or_imprecise_dtypes=True),
              dinc="CONST")
        lb.wait("scalar", "CONST", lb.cnt["CONST"])
        lb.wait("vector", "CONST", lb.cnt["CONST"])

        wpairs = [(Wo[k * P:(k + 1) * P, :], WB1[:, k * FF:k * FF + D]) for k in range(KD)]
        wpairs += [(rw[k * P:(k + 1) * P, :], RWF[:, k * NE:(k + 1) * NE]) for k in range(KD)]
        stage_weights(lb, wpairs, STG)
        lb.wait("vector", "WCST", lb.cnt["WCST"])
        lb.op("vector", lambda v: v.tensor_copy(RWHI[:], RWF[:]))
        lb.op("vector", lambda v: v.tensor_sub(RWLO[:], RWF[:], RWHI[:]), dinc="RWS")
        lb.wait("tensor", "RWS", lb.cnt["RWS"])
        lb.wait("tensor", "WCST", lb.cnt["WCST"])
        lb.wait("tensor", "CONST", lb.cnt["CONST"])

        atct = 0
        for g in range(ng):
            if g > 0:
                lb.wait("sync", "TRP", 8 * g)
            for k in range(KD):
                lb.op("sync", lambda s, g=g, k=k:
                      s.dma_start(WB2[:, k * 512:(k + 1) * 512],
                                  oT[k * P:(k + 1) * P, g * 512:(g + 1) * 512]),
                      inc="XLD", dma=True)
            lb.wait("tensor", "XLD", lb.cnt["XLD"])
            if g > 0:
                lb.wait("vector", "TRP", 8 * g)
            for tt in range(4):
                lb.wait("sync", "LNV", lb.cnt["LNV"])
                lb.wait("sync", "C1ST", lb.cnt["C1ST"])
                lb.op("sync", lambda s, g=g, tt=tt:
                      s.dma_start(X1T[:],
                                  x[g * 512 + tt * P:g * 512 + (tt + 1) * P, :]),
                      inc="XLT", dma=True)
                lb.wait("vector", "XLT", lb.cnt["XLT"])
                if atct >= 1:
                    lb.wait("tensor", "PEV", atct)
                for db in range(2):
                    for k in range(KD):
                        lb.op("tensor", lambda t, k=k, tt=tt, db=db:
                              t.matmul(ps_l2[db][:],
                                       WB2[:, k * 512 + tt * P:k * 512 + (tt + 1) * P],
                                       WB1[:, k * FF + db * 512:k * FF + (db + 1) * 512],
                                       start=(k == 0), stop=(k == KD - 1)),
                              inc="PMM" if k == KD - 1 else None)
                lb.wait("vector", "PMM", lb.cnt["PMM"])
                lb.wait("vector", "XNST", lb.cnt["XNST"])
                lb.op("vector", lambda v: v.tensor_copy(RES[:, 0:512], ps_l2[0][:]))
                lb.op("vector", lambda v: v.tensor_copy(RES[:, 512:D], ps_l2[1][:]),
                      dinc="PEV")
                atct += 1
                emit_ln(lb, RES[:], NRM[:], wk, epsc[:], D)
                lb.op("vector", lambda v:
                      v.tensor_add(X1T[:], X1T[:], NRM[:]), dinc="LNV")
                lb.wait("sync", "LNV", lb.cnt["LNV"])
                lb.op("sync", lambda s, g=g, tt=tt:
                      s.dma_start(x1d[g * 512 + tt * P:g * 512 + (tt + 1) * P, :],
                                  X1T[:]), inc="C1ST", dma=True)
                emit_ln(lb, X1T[:], XN[:, tt * D:(tt + 1) * D], wk, epsc[:], D,
                        out_raw=RES[:])
                lb.op("vector", lambda v, tt=tt:
                      v.tensor_sub(LOG[:, tt * D:(tt + 1) * D], RES[:],
                                   XN[:, tt * D:(tt + 1) * D]), dinc="LNV")
                lb.wait("sync", "LNV", lb.cnt["LNV"])
                lb.op("sync", lambda s, g=g, tt=tt:
                      s.dma_start(xn1fd[g * 512 + tt * P:g * 512 + (tt + 1) * P, :],
                                  RES[:]), inc="XNST", dma=True)
            c1_lnv = lb.cnt["LNV"]
            lb.wait("vector", "C1ST", lb.cnt["C1ST"])
            lb.wait("tensor", "LNV", c1_lnv)
            emit_transposes(lb, lambda k, tt: XN[:, tt * D + k * P:tt * D + (k + 1) * P],
                            lambda k, tt: XT[:, k * 512 + tt * P:k * 512 + (tt + 1) * P],
                            ident, ps_tr, 4)
            lb.wait("sync", "TRE", lb.cnt["TRE"])
            for k in range(KD):
                lb.op("sync", lambda s, g=g, k=k:
                      s.dma_start(xn1Td[k * P:(k + 1) * P, g * 512:(g + 1) * 512],
                                  XT[:, k * 512:(k + 1) * 512]), inc="C1ST", dma=True)
            lb.wait("scalar", "LNV", c1_lnv)
            if g > 0:
                lb.wait("scalar", "RMM", lb.cnt["RMM"])
            for tt in range(4):
                for k in range(KD):
                    lb.op("scalar", lambda ss, k=k, tt=tt:
                          ss.dma_start_transpose(
                              LOGT[:, k * 512 + tt * P:k * 512 + (tt + 1) * P],
                              LOG[:, tt * D + k * P:tt * D + (k + 1) * P]),
                          inc="XTR", dma=True)
            lb.wait("tensor", "TRE", lb.cnt["TRE"])
            lb.wait("tensor", "XTR", lb.cnt["XTR"])
            if g > 0:
                lb.wait("tensor", "RCP", 4 * g)
            for tt in range(4):
                nk = 0
                for k in range(KD):
                    for (lhs_buf, rhs_buf) in ((XT, RWHI), (XT, RWLO), (LOGT, RWHI)):
                        nk += 1
                        lb.op("tensor",
                              lambda t, k=k, tt=tt, lhs_buf=lhs_buf, rhs_buf=rhs_buf,
                              nk=nk:
                              t.matmul(ps_bc[:, tt * NE:(tt + 1) * NE],
                                       lhs_buf[:, k * 512 + tt * P:k * 512 + (tt + 1) * P],
                                       rhs_buf[:, k * NE:(k + 1) * NE],
                                       start=(nk == 1), stop=(nk == 3 * KD)),
                              inc="RMM" if nk == 3 * KD else None)
            lb.wait("vector", "RMM", lb.cnt["RMM"])
            for tt in range(4):
                off = (g * 4 + tt) * NE
                lgt = T8[0]
                lb.op("vector", lambda v, tt=tt, lgt=lgt:
                      v.tensor_copy(lgt[:], ps_bc[:, tt * NE:(tt + 1) * NE]), dinc="RCP")
                lb.op("vector", lambda v, lgt=lgt: v.reduce_max(cols[5][:], lgt[:], AX.X))
                lb.op("vector", lambda v, lgt=lgt:
                      v.tensor_scalar(T8[1][:], lgt[:], cols[5][:], None, ALU.is_equal))
                lb.op("vector", lambda v:
                      v.tensor_scalar(T8[1][:], T8[1][:], -1e9, 1e9, ALU.mult, ALU.add))
                lb.op("vector", lambda v:
                      v.scalar_tensor_tensor(T8[1][:], IOT[:], 1.0, T8[1][:],
                                             ALU.mult, ALU.add))
                lb.op("vector", lambda v:
                      v.tensor_reduce(cols[6][:], T8[1][:], AX.X, ALU.min))
                lb.op("vector", lambda v:
                      v.tensor_scalar(T8[1][:], IOT[:], cols[6][:], None, ALU.is_equal))
                lb.op("vector", lambda v, lgt=lgt:
                      v.scalar_tensor_tensor(T8[2][:], T8[1][:], -1e9, lgt[:],
                                             ALU.mult, ALU.add))
                lb.op("vector", lambda v: v.reduce_max(cols[6][:], T8[2][:], AX.X))
                lb.op("vector", lambda v:
                      v.tensor_scalar(T8[2][:], T8[2][:], cols[6][:], None, ALU.is_equal))
                lb.op("vector", lambda v:
                      v.tensor_scalar(T8[2][:], T8[2][:], -1e9, 1e9, ALU.mult, ALU.add))
                lb.op("vector", lambda v:
                      v.scalar_tensor_tensor(T8[2][:], IOT[:], 1.0, T8[2][:],
                                             ALU.mult, ALU.add))
                lb.op("vector", lambda v:
                      v.tensor_reduce(cols[7][:], T8[2][:], AX.X, ALU.min))
                lb.op("vector", lambda v:
                      v.tensor_scalar(T8[2][:], IOT[:], cols[7][:], None, ALU.is_equal))
                lb.op("vector", lambda v: v.tensor_sub(cols[7][:], cols[6][:], cols[5][:]),
                      dinc="GTV")
                lb.wait("scalar", "GTV", lb.cnt["GTV"])
                lb.op("scalar", lambda s: s.activation(cols[7][:], cols[7][:], AF.Exp),
                      dinc="GTS")
                lb.wait("vector", "GTS", lb.cnt["GTS"])
                lb.op("vector", lambda v:
                      v.tensor_scalar(cols[8][:], cols[7][:], 1.0, None, ALU.add))
                lb.op("vector", lambda v: v.reciprocal(cols[8][:], cols[8][:]))
                lb.op("vector", lambda v:
                      v.tensor_scalar(cols[7][:], cols[8][:], -1.0, 1.0, ALU.mult, ALU.add))
                lb.op("vector", lambda v:
                      v.tensor_scalar(T8[1][:], T8[1][:], cols[8][:], None, ALU.mult))
                lb.op("vector", lambda v, off=off:
                      v.scalar_tensor_tensor(GATES[:, off:off + NE], T8[2][:], cols[7][:],
                                             T8[1][:], ALU.mult, ALU.add), dinc="GTV")

        lb.wait("sync", "C1ST", lb.cnt["C1ST"])
        lb.wait("sync", "XNST", lb.cnt["XNST"])
        for eng in ENGS:
            lb.op(eng, lambda e: e.nop(), inc="PH1")
        for eng in ENGS:
            lb.wait(eng, "PH1", 5)

        l2ct = 0
        for e in range(ne):
            if e > 0:
                lb.wait("sync", "YST", lb.cnt["YST"])
                lb.wait("scalar", "YST", lb.cnt["YST"])
            for k in range(KD):
                lb.op("sync", lambda s, e=e, k=k:
                      s.dma_start(WB1[:, k * FF:(k + 1) * FF], w1b[e, k * P:(k + 1) * P, :]),
                      inc="WLD2", dma=True)
            for f in range(KF):
                lb.op("scalar", lambda s, e=e, f=f:
                      s.dma_start(WB2[:, f * D:(f + 1) * D], w2b[e, f * P:(f + 1) * P, :]),
                      inc="WLD3", dma=True)
            lb.op("sync", lambda s, e=e: s.dma_start(B1C[:], b1L[e]), inc="WLD2", dma=True)
            lb.op("sync", lambda s, e=e: s.dma_start(row1[:], gmr[e]), inc="WLD2", dma=True)
            lb.wait("tensor", "L1EV", lb.cnt["L1EV"])
            for i in range(2):
                lb.wait("tensor", "WLD2", lb.cnt["WLD2"])
                for db in range(2):
                    lb.wait("tensor", "BCV", lb.cnt["BCV"])
                    lb.op("tensor", lambda t, db=db:
                          t.matmul(ps_bc[:], ones1[:], row1[:, db * 512:(db + 1) * 512]),
                          inc="BCM")
                    lb.wait("vector", "BCM", lb.cnt["BCM"])
                    lb.op("vector", lambda v, i=i, db=db:
                          v.tensor_copy(BC[:, i * D + db * 512:i * D + (db + 1) * 512],
                                        ps_bc[:]), dinc="BCV")
                nxt = (btr, b2r)[i]
                lb.wait("sync", "BCV", lb.cnt["BCV"])
                lb.op("sync", lambda s, e=e, nxt=nxt: s.dma_start(row1[:], nxt[e]),
                      inc="WLD2", dma=True)
            lb.wait("scalar", "WLD2", lb.cnt["WLD2"])
            lb.op("scalar", lambda s: s.copy(row1b[:], row1[:]), dinc="B2C")
            lb.wait("tensor", "WLD2", lb.cnt["WLD2"])
            lb.wait("tensor", "WLD3", lb.cnt["WLD3"])
            lb.wait("tensor", "BCV", lb.cnt["BCV"])
            lb.wait("tensor", "B2C", lb.cnt["B2C"])
            lb.wait("vector", "BCV", lb.cnt["BCV"])

            for g in range(ng):
                if g > 0:
                    lb.wait("sync", "L1MM", lb.cnt["L1MM"])
                for k in range(KD):
                    lb.op("sync", lambda s, g=g, k=k:
                          s.dma_start(XT[:, k * 512:(k + 1) * 512],
                                      xn1Td[k * P:(k + 1) * P, g * 512:(g + 1) * 512]),
                          inc="XLD2", dma=True)
                lb.wait("tensor", "XLD2", lb.cnt["XLD2"])
                for f in range(KF):
                    pb = f % 2
                    if lb.cnt["L1MM"] >= 2:
                        lb.wait("tensor", "L1EV", lb.cnt["L1MM"] - 1)
                    for k in range(KD):
                        lb.op("tensor", lambda t, f=f, k=k, pb=pb:
                              t.matmul(ps_l1[pb][:],
                                       WB1[:, k * FF + f * P:k * FF + (f + 1) * P],
                                       XT[:, k * 512:(k + 1) * 512],
                                       start=(k == 0), stop=(k == KD - 1)),
                              inc="L1MM" if k == KD - 1 else None)
                    lb.wait("scalar", "L1MM", lb.cnt["L1MM"])
                    lb.op("scalar", lambda s, f=f, pb=pb:
                          s.activation(HT[:, f * 512:(f + 1) * 512], ps_l1[pb][:],
                                       AF.Gelu, bias=B1C[:, f:f + 1], scale=1.0),
                          dinc="L1EV")
                lb.wait("tensor", "L1EV", lb.cnt["L1EV"])
                for tt in range(4):
                    lb.wait("sync", "L2EV", lb.cnt["L2EV"])
                    lb.op("sync", lambda s, g=g, tt=tt:
                          s.dma_start(X1T[:],
                                      xn1fd[g * 512 + tt * P:g * 512 + (tt + 1) * P, :]),
                          inc="XL2T", dma=True)
                    for db in range(2):
                        bank = l2ct % 4
                        l2ct += 1
                        if l2ct > 4:
                            lb.wait("tensor", "L2EV", l2ct - 4)
                        for f in range(KF):
                            lb.op("tensor", lambda t, f=f, tt=tt, db=db, bank=bank:
                                  t.matmul(ps_l2[bank][:],
                                           HT[:, f * 512 + tt * P:f * 512 + (tt + 1) * P],
                                           WB2[:, f * D + db * 512:f * D + (db + 1) * 512],
                                           start=(f == 0), stop=False))
                        lb.op("tensor", lambda t, db=db, bank=bank:
                              t.matmul(ps_l2[bank][:], ones1b[:],
                                       row1b[:, db * 512:(db + 1) * 512],
                                       start=False, stop=True), inc="L2MM")
                    lb.wait("vector", "L2MM", lb.cnt["L2MM"])
                    lb.wait("vector", "XL2T", lb.cnt["XL2T"])
                    bank0, bank1 = (l2ct - 2) % 4, (l2ct - 1) % 4
                    lb.op("vector", lambda v, bank0=bank0:
                          v.tensor_add(RES[:, 0:512], ps_l2[bank0][:], X1T[:, 0:512]))
                    lb.op("vector", lambda v, bank1=bank1:
                          v.tensor_add(RES[:, 512:D], ps_l2[bank1][:], X1T[:, 512:D]),
                          dinc="L2EV")
                    lb.op("vector", lambda v: v.nop(), inc="L2EV")
                    lb.wait("vector", "YST", lb.cnt["YST"])
                    emit_ln(lb, RES[:], NRM[:], wk, epsm[:], D)
                    off = (g * 4 + tt) * NE
                    lb.op("vector", lambda v:
                          v.tensor_mul(NRM[:], NRM[:], BC[:, 0:D]))
                    lb.op("vector", lambda v:
                          v.tensor_add(NRM[:], NRM[:], BC[:, D:2 * D]))
                    lb.op("vector", lambda v, off=off, e=e:
                          v.tensor_scalar(NRM[:], NRM[:], GATES[:, off + e:off + e + 1],
                                          None, ALU.mult), dinc="YEV")
                    lb.wait("gpsimd", "YEV", lb.cnt["YEV"])
                    if e == 0:
                        lb.op("gpsimd", lambda gg, g=g, tt=tt:
                              gg.dma_start(yacc[g * 512 + tt * P:g * 512 + (tt + 1) * P, :],
                                           NRM[:]), inc="YST", dma=True)
                    else:
                        lb.op("gpsimd", lambda gg, g=g, tt=tt:
                              gg.dma_start(yacc[g * 512 + tt * P:g * 512 + (tt + 1) * P, :],
                                           NRM[:], accum_op=ALU.add), inc="YST", dma=True)

        lb.wait("sync", "YST", lb.cnt["YST"])
        for eng in ENGS:
            lb.op(eng, lambda e: e.nop(), inc="PH2")
        for eng in ENGS:
            lb.wait(eng, "PH2", 5)
        for g in range(ng):
            for tt in range(4):
                lnv_pre = lb.cnt["LNV"]
                lb.wait("sync", "LNV", lnv_pre)
                lb.wait("sync", "C3ST", lb.cnt["C3ST"])
                lb.op("sync", lambda s, g=g, tt=tt:
                      s.dma_start(X1T[:],
                                  x1d[g * 512 + tt * P:g * 512 + (tt + 1) * P, :]),
                      inc="XLD3", dma=True)
                lb.op("sync", lambda s, g=g, tt=tt:
                      s.dma_start(RES[:], yacc[g * 512 + tt * P:g * 512 + (tt + 1) * P, :]),
                      inc="XLD3", dma=True)
                lb.wait("vector", "XLD3", lb.cnt["XLD3"])
                if g > 0:
                    lb.wait("vector", "TRP", lb.cnt["TRP"])
                emit_ln(lb, RES[:], NRM[:], wk, epsc[:], D)
                lb.op("vector", lambda v:
                      v.tensor_add(X1T[:], X1T[:], NRM[:]), dinc="LNV")
                lb.wait("sync", "LNV", lb.cnt["LNV"])
                lb.op("sync", lambda s, g=g, tt=tt:
                      s.dma_start(x2[g * 512 + tt * P:g * 512 + (tt + 1) * P, :],
                                  X1T[:]), inc="C3ST", dma=True)
                emit_ln(lb, X1T[:], XN[:, tt * D:(tt + 1) * D], wk, epsc[:], D)
            c3_lnv = lb.cnt["LNV"]
            lb.wait("tensor", "LNV", c3_lnv)
            lb.wait("vector", "C3ST", lb.cnt["C3ST"])
            emit_transposes(lb, lambda k, tt: XN[:, tt * D + k * P:tt * D + (k + 1) * P],
                            lambda k, tt: XT[:, k * 512 + tt * P:k * 512 + (tt + 1) * P],
                            ident, ps_tr, 4)
            lb.wait("sync", "TRE", lb.cnt["TRE"])
            for k in range(KD):
                lb.op("sync", lambda s, g=g, k=k:
                      s.dma_start(xn2T[k * P:(k + 1) * P, g * 512:(g + 1) * 512],
                                  XT[:, k * 512:(k + 1) * 512]), inc="C3ST", dma=True)
        lb.wait("sync", "C3ST", lb.cnt["C3ST"])
        lb.finish()
    return nc



def build_launch_d(T=T_, nb=B_):
    HB = T // 2
    NGD = HB // 512
    nc = bass.Bass("TRN2", target_bir_lowering=False)
    xn2T = nc.dram_tensor("xn2T", [D, nb * T], BF16, kind="ExternalInput").ap()
    WinT = nc.dram_tensor("WinT", [D, P], F32, kind="ExternalInput").ap()
    WigT = nc.dram_tensor("WigT", [D, P], F32, kind="ExternalInput").ap()
    WrgT = nc.dram_tensor("WrgT", [D, P], F32, kind="ExternalInput").ap()
    big = nc.dram_tensor("big", [P, 1], F32, kind="ExternalInput").ap()
    brg = nc.dram_tensor("brg", [P, 1], F32, kind="ExternalInput").ap()
    lam = nc.dram_tensor("lam", [P, 1], F32, kind="ExternalInput").ap()
    h0 = nc.dram_tensor("h0", [P, 1], F32, kind="ExternalInput").ap()
    hout = nc.dram_tensor("hout", [nb, P, T], F32, kind="ExternalOutput").ap()

    with ExitStack() as ctx:
        E = ctx.enter_context
        lb = LB(nc, ctx)
        WB = E(nc.sbuf_tensor("WB", [P, 3 * KD * P], BF16))
        XTD = E(nc.sbuf_tensor("XTD", [P, KD * 512], BF16))
        A = E(nc.sbuf_tensor("A", [P, HB], F32))
        G = E(nc.sbuf_tensor("G", [P, HB], F32))
        HH = E(nc.sbuf_tensor("HH", [P, HB], F32))
        XP = E(nc.sbuf_tensor("XP", [P, 512], F32))
        I_ = E(nc.sbuf_tensor("I_", [P, 512], F32))
        SQ2 = E(nc.sbuf_tensor("SQ2", [P, 512], F32))
        bigc = E(nc.sbuf_tensor("bigc", [P, 1], F32))
        brgc = E(nc.sbuf_tensor("brgc", [P, 1], F32))
        lamc = E(nc.sbuf_tensor("lamc", [P, 1], F32))
        h0c = E(nc.sbuf_tensor("h0c", [P, 1], F32))
        la8 = E(nc.sbuf_tensor("la8", [P, 1], F32))
        onepc = E(nc.sbuf_tensor("onepc", [P, 1], F32))
        eps8c = E(nc.sbuf_tensor("eps8c", [P, 1], F32))
        hin = E(nc.sbuf_tensor("hin", [P, 1], F32))
        STGD = E(nc.sbuf_tensor("STGD", [P, P], F32))
        ps = [E(nc.psum_tensor(f"psd{i}", [P, 512], F32)) for i in range(6)]

        const_col(lb, onepc, 1.0 + 1e-6)
        const_col(lb, eps8c, 1e-8)
        lb.wait("scalar", "CONST", lb.cnt["CONST"])

        wpairs = []
        for j, W in enumerate((WinT, WigT, WrgT)):
            for k in range(KD):
                wpairs.append((W[k * P:(k + 1) * P, :],
                               WB[:, (j * KD + k) * P:(j * KD + k + 1) * P]))
        stage_weights(lb, wpairs, STGD)
        for src, dst in ((big, bigc), (brg, brgc), (lam, lamc), (h0, h0c)):
            lb.op("sync", lambda s, src=src, dst=dst: s.dma_start(dst[:], src),
                  inc="VLD", dma=True)
        lb.wait("scalar", "VLD", lb.cnt["VLD"])
        lb.op("scalar", lambda s: s.activation(la8[:], lamc[:], AF.Sigmoid), dinc="LA")
        lb.op("scalar", lambda s: s.activation(la8[:], la8[:], AF.Ln, bias=eps8c[:]),
              dinc="LA")
        lb.wait("vector", "LA", lb.cnt["LA"])
        lb.op("vector", lambda v: v.tensor_scalar(la8[:], la8[:], 8.0, None, ALU.mult),
              dinc="LA")
        lb.wait("scalar", "LA", lb.cnt["LA"])
        lb.wait("tensor", "WCST", lb.cnt["WCST"])

        pct = 0
        for hb in range(2 * nb):
            b = hb // 2
            base = b * T + (hb % 2) * HB
            for gg in range(NGD):
                off = gg * 512
                lb.wait("sync", "PRMM", lb.cnt["PRMM"])
                for k in range(KD):
                    lb.op("sync", lambda s, k=k, base=base, off=off:
                          s.dma_start(XTD[:, k * 512:(k + 1) * 512],
                                      xn2T[k * P:(k + 1) * P, base + off:base + off + 512]),
                          inc="XLDD", dma=True)
                lb.wait("tensor", "XLDD", lb.cnt["XLDD"])
                pset = (pct % 2) * 3
                pct += 1
                if pct >= 3:
                    lb.wait("tensor", "PEVD", pct - 2)
                for j in range(3):
                    for k in range(KD):
                        lb.op("tensor", lambda t, j=j, k=k, pset=pset:
                              t.matmul(ps[pset + j][:],
                                       WB[:, (j * KD + k) * P:(j * KD + k + 1) * P],
                                       XTD[:, k * 512:(k + 1) * 512],
                                       start=(k == 0), stop=(k == KD - 1)),
                              inc="PRMM" if k == KD - 1 else None)
                lb.wait("vector", "PRMM", lb.cnt["PRMM"])
                lb.wait("scalar", "PRMM", lb.cnt["PRMM"])
                lb.op("vector", lambda v, pset=pset: v.tensor_copy(XP[:], ps[pset][:]),
                      dinc="ELV")
                lb.op("scalar", lambda s, pset=pset:
                      s.activation(I_[:], ps[pset + 1][:], AF.Sigmoid, bias=bigc[:]))
                lb.op("scalar", lambda s, pset=pset:
                      s.activation(SQ2[:], ps[pset + 2][:], AF.Sigmoid, bias=brgc[:]))
                lb.op("scalar", lambda s, off=off:
                      s.activation(A[:, off:off + 512], SQ2[:], AF.Exp, scale=la8[:]),
                      dinc="ELS")
                lb.op("scalar", lambda s, off=off:
                      s.activation(SQ2[:], A[:, off:off + 512], AF.Square))
                lb.op("scalar", lambda s:
                      s.activation(SQ2[:], SQ2[:], AF.Sqrt, bias=onepc[:], scale=-1.0),
                      dinc="ELS")
                lb.wait("scalar", "ELV", lb.cnt["ELV"])
                lb.op("scalar", lambda s: s.nop(), dinc="PEVD")
                lb.wait("vector", "ELS", lb.cnt["ELS"])
                lb.op("vector", lambda v: v.tensor_mul(SQ2[:], SQ2[:], I_[:]))
                lb.op("vector", lambda v, off=off:
                      v.tensor_mul(G[:, off:off + 512], SQ2[:], XP[:]), dinc="ELV")
            if hb % 2 == 0:
                lb.op("vector", lambda v: v.tensor_copy(hin[:], h0c[:]))
            lb.wait("vector", "ELS", lb.cnt["ELS"])
            lb.op("vector", lambda v:
                  v.tensor_tensor_scan(HH[:], A[:], G[:], hin[:, 0:1],
                                       ALU.mult, ALU.add), dinc="SCN")
            lb.op("vector", lambda v: v.tensor_copy(hin[:], HH[:, HB - 1:HB]),
                  dinc="SCN")
            lb.wait("sync", "SCN", lb.cnt["SCN"])
            lb.op("sync", lambda s, b=b, hb=hb:
                  s.dma_start(hout[b, :, (hb % 2) * HB:(hb % 2) * HB + HB], HH[:]),
                  inc="HST", dma=True)
            lb.wait("scalar", "SCN", lb.cnt["SCN"])
            lb.wait("vector", "HST", lb.cnt["HST"])
        lb.wait("sync", "HST", lb.cnt["HST"])
        lb.finish()
    return nc



def build_launch_f(ng=NG):
    span = ng * 512
    nc = bass.Bass("TRN2", target_bir_lowering=False)
    x2 = nc.dram_tensor("x2", [span, D], F32, kind="ExternalInput").ap()
    hT = nc.dram_tensor("hT", [D, span], F32, kind="ExternalInput").ap()
    nwL = nc.dram_tensor("nwL", [P, KD], F32, kind="ExternalInput").ap()
    WoutT = nc.dram_tensor("WoutT", [D, D], F32, kind="ExternalInput").ap()
    Win = nc.dram_tensor("Win", [D, FM], F32, kind="ExternalInput").ap()
    binc_d = nc.dram_tensor("binc_d", [FM, 1], F32, kind="ExternalInput").ap()
    Ew1 = nc.dram_tensor("Ew1", [FM, D], F32, kind="ExternalInput").ap()
    b1fL = nc.dram_tensor("b1fL", [P, KD], F32, kind="ExternalInput").ap()
    W2blk = nc.dram_tensor("W2blk", [D, FM], F32, kind="ExternalInput").ap()
    b2c_d = nc.dram_tensor("b2c_d", [FM, 1], F32, kind="ExternalInput").ap()
    Wout1 = nc.dram_tensor("Wout1", [FM, D], F32, kind="ExternalInput").ap()
    boutr = nc.dram_tensor("boutr", [1, D], F32, kind="ExternalInput").ap()
    out = nc.dram_tensor("out", [span, D], F32, kind="ExternalOutput").ap()

    with ExitStack() as ctx:
        E = ctx.enter_context
        lb = LB(nc, ctx)
        HS = E(nc.sbuf_tensor("HS", [P, KD * 512], F32))
        HV = E(nc.sbuf_tensor("HV", [P, KD * 512], BF16))
        WoutTb = E(nc.sbuf_tensor("WoutTb", [P, KD * D], BF16))
        Winb = E(nc.sbuf_tensor("Winb", [P, KD * FM], BF16))
        Ew1b = E(nc.sbuf_tensor("Ew1b", [FM, D], BF16))
        W2blkb = E(nc.sbuf_tensor("W2blkb", [P, KD * FM], BF16))
        Wout1b = E(nc.sbuf_tensor("Wout1b", [FM, D], BF16))
        boutb = E(nc.sbuf_tensor("boutb", [1, D], BF16))
        X2T = E(nc.sbuf_tensor("X2T", [P, D], F32))
        X3 = E(nc.sbuf_tensor("X3", [P, D], F32))
        RES = E(nc.sbuf_tensor("RES", [P, D], F32))
        RES4 = E(nc.sbuf_tensor("RES4", [P, 3 * D], F32))
        NRM = E(nc.sbuf_tensor("NRM", [P, D], F32))
        SQ = E(nc.sbuf_tensor("SQ", [P, D], F32))
        SQF = E(nc.sbuf_tensor("SQF", [P, KD * 512], F32))
        XN3 = E(nc.sbuf_tensor("XN3", [P, D], BF16))
        XN3T = E(nc.sbuf_tensor("XN3T", [P, D], BF16))
        sTb = E(nc.sbuf_tensor("sTb", [FM, P], BF16))
        hTb = E(nc.sbuf_tensor("hTb", [P, D], BF16))
        gTb = E(nc.sbuf_tensor("gTb", [FM, P], BF16))
        NWC = E(nc.sbuf_tensor("NWC", [P, KD], F32))
        B1F = E(nc.sbuf_tensor("B1F", [P, KD], F32))
        binc = E(nc.sbuf_tensor("binc", [FM, 1], F32))
        b2c = E(nc.sbuf_tensor("b2c", [FM, 1], F32))
        rstds = E(nc.sbuf_tensor("rstds", [P, 4], F32))
        ident = E(nc.sbuf_tensor("ident", [P, P], BF16))
        ones1b = E(nc.sbuf_tensor("ones1b", [1, P], BF16))
        onescol = E(nc.sbuf_tensor("onescol", [P, 1], F32))
        cols = [E(nc.sbuf_tensor(f"col{i}", [P, 1], F32)) for i in range(5)]
        epsc = E(nc.sbuf_tensor("epsc", [P, 1], F32))
        epsr = E(nc.sbuf_tensor("epsr", [P, 1], F32))
        STG = E(nc.sbuf_tensor("STG", [P, D], F32))
        ps_l2 = [E(nc.psum_tensor(f"ps_l2{i}", [P, 512], F32)) for i in range(2)]
        ps_tr = [E(nc.psum_tensor(f"ps_tr{i}", [P, 4 * P], BF16)) for i in range(2)]
        ps_ss = E(nc.psum_tensor("ps_ss", [P, 4], F32))
        ps_s1 = E(nc.psum_tensor("ps_s1", [FM, P], F32))
        ps_h1 = [E(nc.psum_tensor(f"ps_h1{i}", [P, P], F32)) for i in range(2)]
        wk = {"sq": SQ, "c0": cols[0], "c1": cols[1], "c2": cols[2],
              "c3": cols[3], "c4": cols[4]}

        make_ident(lb, ident)
        const_col(lb, epsc, LN_EPS)
        const_col(lb, epsr, RMS_EPS)
        lb.op("gpsimd", lambda g: g.memset(ones1b[:], 1.0), dinc="CONST")
        lb.op("gpsimd", lambda g: g.memset(onescol[:], 1.0), dinc="CONST")
        lb.wait("scalar", "CONST", lb.cnt["CONST"])
        lb.wait("vector", "CONST", lb.cnt["CONST"])

        wpairs = [(WoutT[k * P:(k + 1) * P, :], WoutTb[:, k * D:(k + 1) * D])
                  for k in range(KD)]
        wpairs += [(Win[k * P:(k + 1) * P, :], Winb[:, k * FM:(k + 1) * FM])
                   for k in range(KD)]
        wpairs += [(Ew1[:, :], Ew1b[:, :])]
        wpairs += [(W2blk[k * P:(k + 1) * P, :], W2blkb[:, k * FM:(k + 1) * FM])
                   for k in range(KD)]
        wpairs += [(Wout1[:, :], Wout1b[:, :])]
        wpairs += [(boutr[:, :], boutb[:, :])]
        stage_weights(lb, wpairs, STG)
        for src, dst in ((nwL, NWC), (b1fL, B1F), (binc_d, binc), (b2c_d, b2c)):
            lb.op("sync", lambda s, src=src, dst=dst: s.dma_start(dst[:], src),
                  inc="VLD", dma=True)
        lb.wait("tensor", "WCST", lb.cnt["WCST"])
        lb.wait("tensor", "CONST", lb.cnt["CONST"])
        lb.wait("vector", "VLD", lb.cnt["VLD"])
        lb.wait("scalar", "VLD", lb.cnt["VLD"])

        l2ct = 0
        for g in range(ng):
            lb.wait("sync", "HVC", lb.cnt["HVC"])
            lb.wait("sync", "SQC", lb.cnt["SQC"])
            lb.wait("vector", "SSK", lb.cnt["SSK"])
            for k in range(KD):
                lb.op("sync", lambda s, g=g, k=k:
                      s.dma_start(HS[:, k * 512:(k + 1) * 512],
                                  hT[k * P:(k + 1) * P, g * 512:(g + 1) * 512]),
                      inc="HLD", dma=True)
            lb.wait("vector", "HLD", lb.cnt["HLD"])
            lb.wait("vector", "WOMM", lb.cnt["WOMM"])
            lb.wait("tensor", "RSV", lb.cnt["RSV"])
            for k in range(KD):
                lb.op("vector", lambda v, k=k:
                      v.tensor_mul(SQF[:, k * 512:(k + 1) * 512],
                                   HS[:, k * 512:(k + 1) * 512],
                                   HS[:, k * 512:(k + 1) * 512]), dinc="SQC")
                lb.op("vector", lambda v, k=k:
                      v.tensor_scalar(HV[:, k * 512:(k + 1) * 512],
                                      HS[:, k * 512:(k + 1) * 512],
                                      NWC[:, k:k + 1], None, ALU.mult), dinc="HVC")
            lb.wait("tensor", "SQC", lb.cnt["SQC"])
            for tt in range(4):
                for k in range(KD):
                    lb.op("tensor", lambda t, k=k, tt=tt:
                          t.matmul(ps_ss[:, tt:tt + 1],
                                   SQF[:, k * 512 + tt * P:k * 512 + (tt + 1) * P],
                                   onescol[:], start=(k == 0), stop=(k == KD - 1)),
                          inc="SSK" if k == KD - 1 else None)
            lb.wait("vector", "SSK", lb.cnt["SSK"])
            lb.op("vector", lambda v:
                  v.tensor_scalar(rstds[:], ps_ss[:], 1.0 / D, None, ALU.mult),
                  dinc="RSV")
            lb.wait("scalar", "RSV", lb.cnt["RSV"])
            lb.op("scalar", lambda s: s.activation(rstds[:], rstds[:], AF.Sqrt,
                                                   bias=epsr[:]), dinc="RSS")
            lb.wait("vector", "RSS", lb.cnt["RSS"])
            lb.op("vector", lambda v: v.reciprocal(rstds[:], rstds[:]), dinc="RSV")
            lb.wait("tensor", "HVC", lb.cnt["HVC"])
            lb.wait("tensor", "RSV", lb.cnt["RSV"])
            for tt in range(4):
                for db in range(2):
                    if l2ct >= 2:
                        lb.wait("tensor", "WOEV", l2ct - 1)
                    for k in range(KD):
                        lb.op("tensor", lambda t, k=k, tt=tt, db=db:
                              t.matmul(ps_l2[db][:],
                                       HV[:, k * 512 + tt * P:k * 512 + (tt + 1) * P],
                                       WoutTb[:, k * D + db * 512:k * D + (db + 1) * 512],
                                       start=(k == 0), stop=(k == KD - 1)),
                              inc="WOMM" if k == KD - 1 else None)
                    l2ct += 1
                lb.wait("vector", "WOMM", lb.cnt["WOMM"])
                lb.op("vector", lambda v, tt=tt:
                      v.tensor_scalar(RES[:, 0:512], ps_l2[0][:],
                                      rstds[:, tt:tt + 1], None, ALU.mult))
                lb.op("vector", lambda v, tt=tt:
                      v.tensor_scalar(RES[:, 512:D], ps_l2[1][:],
                                      rstds[:, tt:tt + 1], None, ALU.mult), dinc="WOEV")
                lb.op("vector", lambda v: v.nop(), inc="WOEV")
                lb.wait("sync", "X3V", lb.cnt["X3V"])
                lb.op("sync", lambda s, g=g, tt=tt:
                      s.dma_start(X2T[:], x2[g * 512 + tt * P:g * 512 + (tt + 1) * P, :]),
                      inc="X2LD", dma=True)
                emit_ln(lb, RES[:], NRM[:, 0:D], wk, epsc[:], D)
                lb.wait("vector", "X2LD", lb.cnt["X2LD"])
                lb.wait("vector", "OST", lb.cnt["OST"])
                lb.op("vector", lambda v: v.tensor_add(X3[:], X2T[:], NRM[:]), dinc="X3V")
                lb.wait("vector", "TRP", lb.cnt["TRP"])
                emit_ln(lb, X3[:], XN3[:], wk, epsc[:], D)
                lb.wait("tensor", "LNV", lb.cnt["LNV"])
                emit_transposes(lb, lambda k, _: XN3[:, k * P:(k + 1) * P],
                                lambda k, _: XN3T[:, k * P:(k + 1) * P],
                                ident, ps_tr, 1)
                lb.wait("tensor", "TRE", lb.cnt["TRE"])
                if g > 0 or tt > 0:
                    lb.wait("tensor", "S1EV", lb.cnt["S1EV"])
                for k in range(KD):
                    lb.op("tensor", lambda t, k=k:
                          t.matmul(ps_s1[:], Winb[:, k * FM:(k + 1) * FM],
                                   XN3T[:, k * P:(k + 1) * P],
                                   start=(k == 0), stop=(k == KD - 1)),
                          inc="S1MM" if k == KD - 1 else None)
                lb.wait("vector", "S1MM", lb.cnt["S1MM"])
                lb.wait("vector", "H1MM", lb.cnt["H1MM"])
                lb.op("vector", lambda v:
                      v.tensor_scalar(sTb[:], ps_s1[:], binc[:, 0:1], None, ALU.add),
                      dinc="S1EV")
                lb.wait("tensor", "S1EV", lb.cnt["S1EV"])
                for j in range(KD):
                    pb = j % 2
                    if j >= 2:
                        lb.wait("tensor", "H1EV", lb.cnt["H1MM"] - 1)
                    lb.op("tensor", lambda t, j=j, pb=pb:
                          t.matmul(ps_h1[pb][:], Ew1b[:, j * P:(j + 1) * P],
                                   sTb[:]), inc="H1MM")
                    lb.wait("scalar", "H1MM", lb.cnt["H1MM"])
                    lb.op("scalar", lambda s, j=j, pb=pb:
                          s.activation(hTb[:, j * P:(j + 1) * P], ps_h1[pb][:],
                                       AF.Relu, bias=B1F[:, j:j + 1]), dinc="H1EV")
                lb.wait("tensor", "H1EV", lb.cnt["H1EV"])
                for j in range(KD):
                    lb.op("tensor", lambda t, j=j:
                          t.matmul(ps_s1[:], W2blkb[:, j * FM:(j + 1) * FM],
                                   hTb[:, j * P:(j + 1) * P],
                                   start=(j == 0), stop=(j == KD - 1)),
                          inc="G1MM" if j == KD - 1 else None)
                lb.wait("vector", "G1MM", lb.cnt["G1MM"])
                lb.wait("vector", "WOMM", lb.cnt["WOMM"])
                lb.op("vector", lambda v:
                      v.tensor_scalar(gTb[:], ps_s1[:], b2c[:, 0:1], None, ALU.add),
                      dinc="S1EV")
                lb.wait("tensor", "S1EV", lb.cnt["S1EV"])
                for db in range(2):
                    if l2ct >= 2:
                        lb.wait("tensor", "WOEV", l2ct - 1)
                    lb.op("tensor", lambda t, db=db:
                          t.matmul(ps_l2[db][:], gTb[:],
                                   Wout1b[:, db * 512:(db + 1) * 512],
                                   start=True, stop=False))
                    lb.op("tensor", lambda t, db=db:
                          t.matmul(ps_l2[db][:], ones1b[:],
                                   boutb[:, db * 512:(db + 1) * 512],
                                   start=False, stop=True), inc="WOMM")
                    l2ct += 1
                lb.wait("vector", "WOMM", lb.cnt["WOMM"])
                lb.op("vector", lambda v: v.tensor_copy(RES[:, 0:512], ps_l2[0][:]))
                lb.op("vector", lambda v: v.tensor_copy(RES[:, 512:D], ps_l2[1][:]),
                      dinc="WOEV")
                lb.op("vector", lambda v: v.nop(), inc="WOEV")
                emit_ln(lb, RES[:], NRM[:, 0:D], wk, epsc[:], D)
                lb.op("vector", lambda v: v.tensor_add(X3[:], X3[:], NRM[:]),
                      dinc="X3V")
                lb.wait("sync", "X3V", lb.cnt["X3V"])
                lb.op("sync", lambda s, g=g, tt=tt:
                      s.dma_start(out[g * 512 + tt * P:g * 512 + (tt + 1) * P, :],
                                  X3[:]), inc="OST", dma=True)
        lb.wait("sync", "OST", lb.cnt["OST"])
        lb.finish()
    return nc



def _shard_tokens(x_full):
    out = []
    for c in range(NCORE):
        b, s = c // 4, c % 4
        out.append(np.ascontiguousarray(x_full[b, s * SPAN:(s + 1) * SPAN, :]))
    return out


def _cc(a):
    return np.ascontiguousarray(a)


def kernel(**inputs):
    out, _ = _forward({k: np.asarray(v) for k, v in inputs.items()})
    return out


def _forward(ii):
    x = ii["L0_attn_x"].astype(np.float32)
    spans = _shard_tokens(x)
    cores = list(range(NCORE))

    ncW = build_launch_w()
    w1 = ii["L0_ffn_w1"].astype(np.float32)
    w2 = ii["L0_ffn_w2"].astype(np.float32)
    rW = run_bass_kernel_spmd(
        ncW, [{"w1": _cc(w1[c]), "w2": _cc(w2[c])} for c in cores], core_ids=cores)
    w1b_all = np.stack([np.asarray(rW.results[c]["w1o"]) for c in cores])
    w2b_all = np.stack([np.asarray(rW.results[c]["w2o"]) for c in cores])

    ncA = build_launch_a()
    inA = [{"x": spans[c], "Wq": ii["L0_attn_W_q"], "Wk": ii["L0_attn_W_k"],
            "Wv": ii["L0_attn_W_v"]} for c in cores]
    rA = run_bass_kernel_spmd(ncA, inA, core_ids=cores)
    qT_full = [np.concatenate([np.asarray(rA.results[4 * b + s]["qT"]) for s in range(4)],
                              axis=1) for b in range(B_)]
    kT_full = [np.concatenate([np.asarray(rA.results[4 * b + s]["kT"]) for s in range(4)],
                              axis=1) for b in range(B_)]
    qm_full = [np.concatenate([np.asarray(rA.results[4 * b + s]["qm"]) for s in range(4)],
                              axis=0) for b in range(B_)]
    km_full = [np.concatenate([np.asarray(rA.results[4 * b + s]["km"]) for s in range(4)],
                              axis=0) for b in range(B_)]
    vm_full = [np.concatenate([np.asarray(rA.results[4 * b + s]["vm"]) for s in range(4)],
                              axis=0) for b in range(B_)]

    ncB = build_launch_b()
    inB = []
    for c in cores:
        b = c // 4
        hs = [4 * (c % 4) + j for j in range(4)]
        inB.append({
            "qTh": np.stack([qT_full[b][64 * h:64 * (h + 1)] for h in hs]),
            "kTh": np.stack([kT_full[b][64 * h:64 * (h + 1)] for h in hs]),
            "qmh": np.stack([_cc(qm_full[b][:, 64 * h:64 * (h + 1)]) for h in hs]),
            "kmh": np.stack([_cc(km_full[b][:, 64 * h:64 * (h + 1)]) for h in hs]),
            "vmh": np.stack([_cc(vm_full[b][:, 64 * h:64 * (h + 1)]) for h in hs]),
        })
    rB = run_bass_kernel_spmd(ncB, inB, core_ids=cores)
    oT_full = [np.empty((D, T_), ml_dtypes.bfloat16) for _ in range(B_)]
    for c in cores:
        b = c // 4
        for j in range(4):
            h = 4 * (c % 4) + j
            oT_full[b][64 * h:64 * (h + 1)] = np.asarray(rB.results[c]["oTh"])[j]

    ncC = build_launch_c()
    b1 = ii["L0_ffn_b1"].astype(np.float32)
    inC = []
    for c in cores:
        b, s = c // 4, c % 4
        inC.append({
            "x": spans[c],
            "oT": _cc(oT_full[b][:, s * SPAN:(s + 1) * SPAN]),
            "Wo": ii["L0_attn_W_o"], "rw": ii["L0_ffn_router_w"],
            "w1b": w1b_all,
            "b1L": _cc(b1.reshape(NE, KF, P).transpose(0, 2, 1)),
            "w2b": w2b_all,
            "b2r": ii["L0_ffn_b2"].reshape(NE, 1, D),
            "gmr": ii["L0_ffn_gamma"].reshape(NE, 1, D),
            "btr": ii["L0_ffn_beta"].reshape(NE, 1, D),
        })
    rC = run_bass_kernel_spmd(ncC, inC, core_ids=cores)
    x2_c = [np.asarray(rC.results[c]["x2"]) for c in cores]
    xn2T_full = np.concatenate(
        [np.concatenate([np.asarray(rC.results[4 * b + s]["xn2T"]) for s in range(4)],
                        axis=1) for b in range(B_)], axis=1)

    ncD = build_launch_d()
    WinT = _cc(ii["L1_attn_W_in"].T.astype(np.float32))
    WigT = _cc(ii["L1_attn_W_ig"].T.astype(np.float32))
    WrgT = _cc(ii["L1_attn_W_rg"].T.astype(np.float32))
    h0v = ii["L1_attn_h0"].reshape(D).astype(np.float32)
    inD = []
    for c in cores:
        ch = slice(P * c, P * (c + 1))
        inD.append({
            "xn2T": xn2T_full,
            "WinT": _cc(WinT[:, ch]), "WigT": _cc(WigT[:, ch]),
            "WrgT": _cc(WrgT[:, ch]),
            "big": _cc(ii["L1_attn_b_ig"][ch].reshape(P, 1).astype(np.float32)),
            "brg": _cc(ii["L1_attn_b_rg"][ch].reshape(P, 1).astype(np.float32)),
            "lam": _cc(ii["L1_attn_lam"][ch].reshape(P, 1).astype(np.float32)),
            "h0": _cc(h0v[ch].reshape(P, 1)),
        })
    rD = run_bass_kernel_spmd(ncD, inD, core_ids=cores)
    hT_full = [np.empty((D, T_), np.float32) for _ in range(B_)]
    for c in cores:
        for b in range(B_):
            hT_full[b][P * c:P * (c + 1)] = np.asarray(rD.results[c]["hout"])[b]

    ncF = build_launch_f()
    W1 = ii["L1_ffn_W1"].astype(np.float32)
    W2 = ii["L1_ffn_W2"].astype(np.float32)
    b1f = ii["L1_ffn_b1"].astype(np.float32).reshape(FM * FI)
    Ew1 = np.zeros((FM, D), np.float32)
    W2blk = np.zeros((D, FM), np.float32)
    for mi in range(FM):
        Ew1[mi, mi * FI:(mi + 1) * FI] = W1[mi]
        W2blk[mi * FI:(mi + 1) * FI, mi] = W2[mi]
    inF = []
    for c in cores:
        b, s = c // 4, c % 4
        inF.append({
            "x2": x2_c[c],
            "hT": _cc(hT_full[b][:, s * SPAN:(s + 1) * SPAN]),
            "nwL": _cc(ii["L1_attn_norm_w"].reshape(KD, P).T.astype(np.float32)),
            "WoutT": _cc(ii["L1_attn_W_out"].T.astype(np.float32)),
            "Win": ii["L1_ffn_Win"].astype(np.float32),
            "binc_d": ii["L1_ffn_bin_"].reshape(FM, 1).astype(np.float32),
            "Ew1": Ew1,
            "b1fL": _cc(b1f.reshape(KD, P).T),
            "W2blk": W2blk,
            "b2c_d": ii["L1_ffn_b2"].reshape(FM, 1).astype(np.float32),
            "Wout1": ii["L1_ffn_Wout"].astype(np.float32),
            "boutr": ii["L1_ffn_bout"].reshape(1, D).astype(np.float32),
        })
    rF = run_bass_kernel_spmd(ncF, inF, core_ids=cores)
    out = np.empty((B_, T_, D), np.float32)
    for c in cores:
        b, s = c // 4, c % 4
        out[b, s * SPAN:(s + 1) * SPAN] = np.asarray(rF.results[c]["out"])
    inter = {"oT_full": oT_full, "x2_c": x2_c, "hT_full": hT_full,
             "qT_full": qT_full, "kT_full": kT_full, "vm_full": vm_full,
             "xn2T_full": xn2T_full}
    return out, inter


# version 06
# speedup vs baseline: 1.0479x; 1.0479x over previous
import numpy as np
import ml_dtypes
from collections import defaultdict
from contextlib import ExitStack

import concourse.bass as bass
import concourse.mybir as mybir
from concourse.bass_utils import run_bass_kernel_spmd

F32 = mybir.dt.float32
BF16 = mybir.dt.bfloat16
AF = mybir.ActivationFunctionType
ALU = mybir.AluOpType
AX = mybir.AxisListType

P = 128
D = 1024
KD = D // P
NCORE = 8
SPAN = 4096
NG = SPAN // 512
B_, T_ = 2, 16384
H, DK, DV = 16, 64, 64
NE, FF = 8, 4096
KF = FF // P
LN_EPS = 1e-5
MOE_EPS = 1e-6
RMS_EPS = 1e-6
FM, FI = 64, 16

ENGS = ("sync", "tensor", "vector", "scalar", "gpsimd")


class LB:

    def __init__(self, nc, ctx):
        self.nc = nc
        self.ctx = ctx
        self.q = {e: [] for e in ENGS}
        self.sems = {}
        self.cnt = defaultdict(int)

    def sem(self, name):
        if name not in self.sems:
            self.sems[name] = self.ctx.enter_context(self.nc.semaphore(name))
        return self.sems[name]

    def op(self, eng, fn, inc=None, dinc=None, dma=False):
        if inc is None:
            inc = dinc
        drain = (eng in ("vector", "scalar", "gpsimd")) and not dma
        if inc is not None:
            s = self.sem(inc)
            self.cnt[inc] += 16 if dma else 1
        else:
            s = None
        d = 16 if dma else 1

        def run(e, fn=fn, s=s, d=d, drain=drain):
            instr = fn(e)
            if drain:
                e.drain()
                if s is not None:
                    e.nop().then_inc(s, d)
            elif s is not None:
                instr.then_inc(s, d)
        self.q[eng].append(run)

    def wait(self, eng, sname, val):
        if val <= 0:
            return
        s = self.sem(sname)
        self.q[eng].append(lambda e, s=s, v=val: e.wait_ge(s, v))

    def finish(self):
        block = self.ctx.enter_context(self.nc.Block())
        for eng in ENGS:
            ops = self.q[eng]
            if not ops:
                continue
            def section(e, ops=ops):
                for f in ops:
                    f(e)
            getattr(block, eng)(section)


def make_ident(lb, ident_bf):
    def f(g):
        g.memset(ident_bf[:], 1.0)
        g.drain()
        return g.affine_select(out=ident_bf[:], in_=ident_bf[:],
                               compare_op=ALU.is_equal, fill=0.0, base=0,
                               pattern=[[-1, P]], channel_multiplier=1)
    lb.op("gpsimd", f, dinc="CONST")


def make_mask(lb, mask, strict):
    cmp = ALU.is_gt if strict else ALU.is_ge
    def f(g):
        g.memset(mask[:], 1.0)
        g.drain()
        return g.affine_select(out=mask[:], in_=mask[:], compare_op=cmp,
                               fill=0.0, base=0, pattern=[[1, P]],
                               channel_multiplier=-1)
    lb.op("gpsimd", f, dinc="CONST")


def const_col(lb, nc_sbuf_col, value):
    lb.op("gpsimd", lambda g: g.memset(nc_sbuf_col[:], value), dinc="CONST")


def emit_ln(lb, xin, xout, wk, eps_col, nfree, out_raw=None):
    sq, c0, c1, c2, c3, c4 = wk["sq"], wk["c0"], wk["c1"], wk["c2"], wk["c3"], wk["c4"]
    lb.op("vector", lambda v: v.tensor_mul(sq[:, 0:nfree], xin, xin))
    lb.op("vector", lambda v: v.reduce_sum(c0[:], xin, AX.X))
    lb.op("vector", lambda v: v.reduce_sum(c1[:], sq[:, 0:nfree], AX.X))
    lb.op("vector", lambda v: v.tensor_scalar(c0[:], c0[:], 1.0 / nfree, None, ALU.mult))
    lb.op("vector", lambda v: v.tensor_scalar(c1[:], c1[:], 1.0 / nfree, None, ALU.mult))
    lb.op("vector", lambda v: v.tensor_mul(c2[:], c0[:], c0[:]))
    lb.op("vector", lambda v: v.tensor_sub(c1[:], c1[:], c2[:]), dinc="LNV")
    lb.wait("scalar", "LNV", lb.cnt["LNV"])
    lb.op("scalar", lambda s: s.activation(c2[:], c1[:], AF.Sqrt, bias=eps_col), dinc="LNS")
    lb.wait("vector", "LNS", lb.cnt["LNS"])
    lb.op("vector", lambda v: v.reciprocal(c3[:], c2[:]))
    lb.op("vector", lambda v: v.tensor_scalar(c4[:], c0[:], c3[:], -1.0, ALU.mult, ALU.mult))
    lb.op("vector", lambda v: v.tensor_scalar(xout, xin, c3[:], c4[:], ALU.mult, ALU.add),
          dinc="LNV")
    if out_raw is not None:
        lb.op("vector", lambda v: v.tensor_scalar(out_raw, xin, c3[:], c4[:], ALU.mult, ALU.add),
              dinc="LNV")


def emit_transposes(lb, get_src, xnT_dst, ident_bf, pst2, nt):
    items = [(k, tt) for k in range(KD) for tt in range(nt)]
    assert len(items) % 4 == 0
    nbatch = len(items) // 4
    for bi in range(nbatch):
        pp = bi % 2
        lb.wait("tensor", "TRE", lb.cnt["TRE"] - 1)
        for j in range(4):
            k, tt = items[bi * 4 + j]
            lb.op("tensor",
                  lambda t, k=k, tt=tt, pp=pp, j=j:
                  t.transpose(pst2[pp][:, j * P:(j + 1) * P], get_src(k, tt),
                              ident_bf[:]),
                  inc="TRP" if j == 3 else None)
        lb.wait("vector", "TRP", lb.cnt["TRP"])
        for j in range(4):
            k, tt = items[bi * 4 + j]
            lb.op("vector",
                  lambda v, k=k, tt=tt, pp=pp, j=j:
                  v.tensor_copy(xnT_dst(k, tt), pst2[pp][:, j * P:(j + 1) * P]),
                  dinc="TRE" if j == 3 else None)


def stage_weights(lb, pairs, stg, ld_sem="WLD", cast_sem="WCST"):
    for i, (src, dst) in enumerate(pairs):
        n = src.shape[-1]
        pp = src.shape[-2]
        if i > 0:
            lb.wait("sync", cast_sem, i)
        lb.op("sync", lambda s, src=src, n=n, pp=pp: s.dma_start(stg[0:pp, 0:n], src),
              inc=ld_sem, dma=True)
        lb.wait("gpsimd", ld_sem, lb.cnt[ld_sem])
        lb.op("gpsimd", lambda g, dst=dst, n=n, pp=pp: g.tensor_copy(dst, stg[0:pp, 0:n]),
              dinc=cast_sem)



def build_launch_a():
    nc = bass.Bass("TRN2", target_bir_lowering=False)
    x = nc.dram_tensor("x", [SPAN, D], F32, kind="ExternalInput").ap()
    Wq = nc.dram_tensor("Wq", [D, D], F32, kind="ExternalInput").ap()
    Wk = nc.dram_tensor("Wk", [D, D], F32, kind="ExternalInput").ap()
    Wv = nc.dram_tensor("Wv", [D, D], F32, kind="ExternalInput").ap()
    qT = nc.dram_tensor("qT", [D, SPAN], BF16, kind="ExternalOutput").ap()
    kT = nc.dram_tensor("kT", [D, SPAN], BF16, kind="ExternalOutput").ap()
    qm = nc.dram_tensor("qm", [SPAN, D], BF16, kind="ExternalOutput").ap()
    km = nc.dram_tensor("km", [SPAN, D], BF16, kind="ExternalOutput").ap()
    vm = nc.dram_tensor("vm", [SPAN, D], BF16, kind="ExternalOutput").ap()

    with ExitStack() as ctx:
        E = ctx.enter_context
        lb = LB(nc, ctx)
        wqb = E(nc.sbuf_tensor("wqb", [P, KD * D], BF16))
        wkb = E(nc.sbuf_tensor("wkb", [P, KD * D], BF16))
        wvb = E(nc.sbuf_tensor("wvb", [P, KD * D], BF16))
        wstg = E(nc.sbuf_tensor("wstg", [P, D], F32))
        xs = [E(nc.sbuf_tensor(f"xs{t}", [P, D], F32)) for t in range(4)]
        xng = [E(nc.sbuf_tensor(f"xng{t}", [P, D], BF16)) for t in range(4)]
        xnT = [E(nc.sbuf_tensor(f"xnT{k}", [P, 512], BF16)) for k in range(KD)]
        ident = E(nc.sbuf_tensor("ident", [P, P], BF16))
        sq = E(nc.sbuf_tensor("sq", [P, D], F32))
        cols = [E(nc.sbuf_tensor(f"col{i}", [P, 1], F32)) for i in range(5)]
        epsc = E(nc.sbuf_tensor("epsc", [P, 1], F32))
        ev = [E(nc.sbuf_tensor(f"ev{i}", [P, 512], BF16)) for i in range(4)]
        pst2 = [E(nc.psum_tensor(f"pst{i}", [P, 4 * P], BF16)) for i in range(2)]
        psp = [E(nc.psum_tensor(f"psp{i}", [P, 512], F32)) for i in range(4)]
        wk = {"sq": sq, "c0": cols[0], "c1": cols[1], "c2": cols[2],
              "c3": cols[3], "c4": cols[4]}

        make_ident(lb, ident)
        const_col(lb, epsc, LN_EPS)
        lb.wait("scalar", "CONST", lb.cnt["CONST"])

        wpairs = []
        for k in range(KD):
            wpairs.append((Wq[k * P:(k + 1) * P, :], wqb[:, k * D:(k + 1) * D]))
            wpairs.append((Wk[k * P:(k + 1) * P, :], wkb[:, k * D:(k + 1) * D]))
            wpairs.append((Wv[k * P:(k + 1) * P, :], wvb[:, k * D:(k + 1) * D]))
        stage_weights(lb, wpairs, wstg)
        n_wcast = lb.cnt["WCST"]

        nmm = 0
        for g in range(NG):
            if g > 0:
                lb.wait("sync", "LNV", lnv_after)
            for tt in range(4):
                lb.op("sync", lambda s, g=g, tt=tt:
                      s.dma_start(xs[tt][:], x[g * 512 + tt * P:g * 512 + (tt + 1) * P, :]),
                      inc="XLD", dma=True)
            lb.wait("vector", "XLD", lb.cnt["XLD"])
            lb.wait("vector", "TRP", 8 * g)
            for tt in range(4):
                emit_ln(lb, xs[tt][:], xng[tt][:], wk, epsc[:], D)
            lnv_after = lb.cnt["LNV"]
            if g == 0:
                lb.wait("tensor", "WCST", n_wcast)
                lb.wait("tensor", "CONST", lb.cnt["CONST"])
            lb.wait("tensor", "LNV", lnv_after)
            emit_transposes(lb, lambda k, tt: xng[tt][:, k * P:(k + 1) * P],
                            lambda k, tt: xnT[k][:, tt * P:(tt + 1) * P],
                            ident, pst2, 4)
            teready = lb.cnt["TRE"]
            lb.wait("tensor", "TRE", teready)

            def proj(mk_mm, dst_dma):
                nonlocal nmm
                bank = nmm % 4
                if nmm >= 4:
                    lb.wait("tensor", "PEV", nmm - 3)
                for k in range(KD):
                    lb.op("tensor", mk_mm(k, bank),
                          inc="PMM" if k == KD - 1 else None)
                lb.wait("scalar", "PMM", lb.cnt["PMM"])
                if nmm >= 4:
                    lb.wait("scalar", f"ST{bank}", lb.cnt[f"ST{bank}"])
                lb.op("scalar", lambda s, bank=bank: s.copy(ev[bank][:], psp[bank][:]),
                      dinc="PEV")
                lb.wait("sync", "PEV", lb.cnt["PEV"])
                lb.op("sync", dst_dma(bank), inc=f"ST{bank}", dma=True)
                nmm += 1

            for (wb, dstT) in ((wqb, qT), (wkb, kT)):
                for hd in range(KD):
                    proj(lambda k, bank, wb=wb, hd=hd:
                         (lambda t: t.matmul(psp[bank][:],
                                             wb[:, k * D + hd * P:k * D + (hd + 1) * P],
                                             xnT[k][:], start=(k == 0), stop=(k == KD - 1))),
                         lambda bank, dstT=dstT, hd=hd, g=g:
                         (lambda s: s.dma_start(
                             dstT[hd * P:(hd + 1) * P, g * 512:(g + 1) * 512], ev[bank][:])))
            for (wb, dstm) in ((wqb, qm), (wkb, km), (wvb, vm)):
                for tt in range(4):
                    for db in range(2):
                        proj(lambda k, bank, wb=wb, tt=tt, db=db:
                             (lambda t: t.matmul(psp[bank][:],
                                                 xnT[k][:, tt * P:(tt + 1) * P],
                                                 wb[:, k * D + db * 512:k * D + (db + 1) * 512],
                                                 start=(k == 0), stop=(k == KD - 1))),
                             lambda bank, dstm=dstm, g=g, tt=tt, db=db:
                             (lambda s: s.dma_start(
                                 dstm[g * 512 + tt * P:g * 512 + (tt + 1) * P,
                                      db * 512:(db + 1) * 512], ev[bank][:])))
        for bank in range(4):
            lb.wait("sync", f"ST{bank}", lb.cnt[f"ST{bank}"])
        lb.finish()
    return nc



def build_launch_b(T=T_):
    NH = 4
    NCH = T // P
    nc = bass.Bass("TRN2", target_bir_lowering=False)
    qTh = nc.dram_tensor("qTh", [NH, DK, T], BF16, kind="ExternalInput").ap()
    kTh = nc.dram_tensor("kTh", [NH, DK, T], BF16, kind="ExternalInput").ap()
    qmh = nc.dram_tensor("qmh", [NH, T, DK], BF16, kind="ExternalInput").ap()
    kmh = nc.dram_tensor("kmh", [NH, T, DK], BF16, kind="ExternalInput").ap()
    vmh = nc.dram_tensor("vmh", [NH, T, DV], BF16, kind="ExternalInput").ap()
    oTh = nc.dram_tensor("oTh", [NH, DV, T], BF16, kind="ExternalOutput").ap()

    with ExitStack() as ctx:
        E = ctx.enter_context
        lb = LB(nc, ctx)
        qTc = [[E(nc.sbuf_tensor(f"qTc{h}_{p}", [DK, P], BF16)) for p in range(2)]
               for h in range(NH)]
        kTc = [[E(nc.sbuf_tensor(f"kTc{h}_{p}", [DK, P], BF16)) for p in range(2)]
               for h in range(NH)]
        qmc = [[E(nc.sbuf_tensor(f"qmc{h}_{p}", [P, DK], BF16)) for p in range(2)]
               for h in range(NH)]
        kmc = [[E(nc.sbuf_tensor(f"kmc{h}_{p}", [P, DK], BF16)) for p in range(2)]
               for h in range(NH)]
        vmc = [[E(nc.sbuf_tensor(f"vmc{h}_{p}", [P, DV], BF16)) for p in range(2)]
               for h in range(NH)]
        oTp = [E(nc.sbuf_tensor(f"oTp{i}", [P, T], BF16)) for i in range(2)]
        Sf = [E(nc.sbuf_tensor(f"Sf{h}", [DK, DK], F32)) for h in range(NH)]
        Cf = [E(nc.sbuf_tensor(f"Cf{h}", [DK, DV], F32)) for h in range(NH)]
        Gf = [E(nc.sbuf_tensor(f"Gf{h}", [DK, DV], F32)) for h in range(NH)]
        Sb = [E(nc.sbuf_tensor(f"Sb{h}", [DK, DK], BF16)) for h in range(NH)]
        bBtp = [E(nc.sbuf_tensor(f"bBtp{h}", [P, P], BF16)) for h in range(NH)]
        bBtm = [E(nc.sbuf_tensor(f"bBtm{h}", [P, P], BF16)) for h in range(NH)]
        bBqm = [E(nc.sbuf_tensor(f"bBqm{h}", [P, P], BF16)) for h in range(NH)]
        bPT = [E(nc.sbuf_tensor(f"bPT{h}", [DK, P], BF16)) for h in range(NH)]
        bZb = [E(nc.sbuf_tensor(f"bZb{h}", [DK, DV], BF16)) for h in range(NH)]
        bMt = [E(nc.sbuf_tensor(f"bMt{h}", [P, P], BF16)) for h in range(NH)]
        bKKf = [E(nc.sbuf_tensor(f"bKKf{h}", [DK, DK], F32)) for h in range(NH)]
        bInner = [E(nc.sbuf_tensor(f"bInner{h}", [P, DV], BF16)) for h in range(NH)]
        m_ge = E(nc.sbuf_tensor("m_ge", [P, P], F32))
        m_gt = E(nc.sbuf_tensor("m_gt", [P, P], F32))
        pb1 = [E(nc.psum_tensor(f"pb1_{h}", [P, 512], F32)) for h in range(NH)]
        pb2 = [E(nc.psum_tensor(f"pb2_{h}", [P, 512], F32)) for h in range(NH)]

        make_mask(lb, m_ge, strict=False)
        make_mask(lb, m_gt, strict=True)
        lb.wait("vector", "CONST", lb.cnt["CONST"])
        vb9 = [0] * NH
        sb10 = [0] * NH
        for h in range(NH):
            lb.op("vector", lambda v, h=h: v.memzero(Sf[h][:]))
            lb.op("vector", lambda v, h=h: v.memzero(Cf[h][:]))
            lb.op("vector", lambda v, h=h: v.memzero(Gf[h][:]), dinc="VB")
            vb9[h] = lb.cnt["VB"]
            lb.op("scalar", lambda s, h=h: s.memzero(Sb[h][:]), dinc="SB2")
            sb10[h] = lb.cnt["SB2"]

        def emit_loads(ch, h):
            p = ch % 2
            sem = f"LD{h}_{p}"
            lb.op("sync", lambda s, ch=ch, h=h, p=p:
                  s.dma_start(qTc[h][p][:], qTh[h, :, ch * P:(ch + 1) * P]),
                  inc=sem, dma=True)
            lb.op("sync", lambda s, ch=ch, h=h, p=p:
                  s.dma_start(kTc[h][p][:], kTh[h, :, ch * P:(ch + 1) * P]),
                  inc=sem, dma=True)
            lb.op("scalar", lambda s, ch=ch, h=h, p=p:
                  s.dma_start(qmc[h][p][:], qmh[h, ch * P:(ch + 1) * P, :]),
                  inc=sem, dma=True)
            lb.op("scalar", lambda s, ch=ch, h=h, p=p:
                  s.dma_start(kmc[h][p][:], kmh[h, ch * P:(ch + 1) * P, :]),
                  inc=sem, dma=True)
            lb.op("sync", lambda s, ch=ch, h=h, p=p:
                  s.dma_start(vmc[h][p][:], vmh[h, ch * P:(ch + 1) * P, :]),
                  inc=sem, dma=True)

        for h in range(NH):
            emit_loads(0, h)
            emit_loads(1, h)

        tb4 = [0] * NH
        vb5 = [0] * NH
        tb6 = [0] * NH
        vb7 = [0] * NH
        tb8 = [0] * NH
        regions = []
        for h in range(NH):
            regions.append({
                "A": pb1[h][:, 0:P],
                "B": pb1[h][:, P:2 * P],
                "PT": pb1[h][0:DK, 2 * P:3 * P],
                "Z": pb1[h][0:DK, 3 * P:3 * P + DV],
                "I": pb1[h][:, 3 * P + DV:3 * P + 2 * DV],
                "M": pb2[h][:, 0:P],
                "O": pb2[h][0:DV, P:2 * P],
                "S": pb2[h][0:DK, 2 * P:2 * P + 3 * DK],
                "G": pb2[h][0:DK, 2 * P + 3 * DK:2 * P + 4 * DK],
            })

        for ch in range(NCH):
            p = ch % 2
            for h in range(NH):
                r = regions[h]
                QT = qTc[h][p][:]
                KT = kTc[h][p][:]
                lb.wait("tensor", f"LD{h}_{p}", lb.cnt[f"LD{h}_{p}"])
                lb.wait("tensor", "VB", vb9[h])
                lb.wait("tensor", "SB2", sb10[h])
                lb.op("tensor", lambda t, KT=KT, QT=QT, r=r: t.matmul(r["A"], KT, QT))
                lb.op("tensor", lambda t, KT=KT, QT=QT, r=r: t.matmul(r["B"], QT, KT))
                lb.op("tensor", lambda t, QT=QT, h=h, r=r: t.matmul(r["PT"], Sb[h][:], QT))
                lb.op("tensor", lambda t, h=h, r=r:
                      t.matmul(r["Z"], Sf[h][:], Cf[h][:]), inc="TB")
                tb4[h] = lb.cnt["TB"]
            for h in range(NH):
                r = regions[h]
                lb.wait("vector", "TB", tb4[h])
                lb.op("vector", lambda v, h=h, r=r: v.tensor_copy(bBtp[h][:], r["A"]))
                lb.op("vector", lambda v, h=h, r=r:
                      v.tensor_mul(bBtm[h][:], r["A"], m_ge[:]))
                lb.op("vector", lambda v, h=h, r=r:
                      v.tensor_mul(bBqm[h][:], r["B"], m_gt[:]))
                lb.op("vector", lambda v, h=h, r=r: v.tensor_copy(bPT[h][:], r["PT"]))
                lb.op("vector", lambda v, h=h, r=r:
                      v.tensor_sub(bZb[h][:], r["Z"], Gf[h][:]), dinc="VB")
                vb5[h] = lb.cnt["VB"]
            for h in range(NH):
                r = regions[h]
                QT = qTc[h][p][:]
                Km = kmc[h][p][:]
                Qm = qmc[h][p][:]
                Vm = vmc[h][p][:]
                lb.wait("tensor", "VB", vb5[h])
                lb.op("tensor", lambda t, h=h, r=r:
                      t.matmul(r["M"], bBtm[h][:], bBtp[h][:], start=True, stop=False))
                lb.op("tensor", lambda t, h=h, QT=QT, r=r:
                      t.matmul(r["M"], QT, bPT[h][:], start=False, stop=True))
                lb.op("tensor", lambda t, h=h, Vm=Vm, r=r:
                      t.matmul(r["I"], bBqm[h][:], Vm))
                lb.op("tensor", lambda t, Km=Km, r=r:
                      t.matmul(r["S"][:, 0:DK], Km, Km))
                lb.op("tensor", lambda t, Qm=Qm, Vm=Vm, r=r:
                      t.matmul(r["S"][:, DK:2 * DK], Qm, Vm), inc="TB")
                tb6[h] = lb.cnt["TB"]
            for h in range(NH):
                r = regions[h]
                lb.wait("vector", "TB", tb6[h])
                lb.op("vector", lambda v, h=h, r=r:
                      v.tensor_mul(bMt[h][:], r["M"], m_ge[:]))
                lb.op("vector", lambda v, h=h, r=r:
                      v.tensor_copy(bKKf[h][:], r["S"][:, 0:DK]))
                lb.op("vector", lambda v, h=h, r=r:
                      v.tensor_copy(bInner[h][:], r["I"]), dinc="VB")
                vb7[h] = lb.cnt["VB"]
            for h in range(NH):
                r = regions[h]
                QT = qTc[h][p][:]
                Km = kmc[h][p][:]
                lb.wait("tensor", "VB", vb7[h])
                Vm8 = vmc[h][p][:]
                lb.op("tensor", lambda t, h=h, r=r, Vm8=Vm8:
                      t.matmul(r["O"], Vm8, bMt[h][:], start=True, stop=False))
                lb.op("tensor", lambda t, h=h, QT=QT, r=r:
                      t.matmul(r["O"], bZb[h][:], QT, start=False, stop=True))
                lb.op("tensor", lambda t, h=h, Km=Km, r=r:
                      t.matmul(r["S"][:, 2 * DK:3 * DK], Km, bInner[h][:]))
                lb.op("tensor", lambda t, h=h, r=r:
                      t.matmul(r["G"], bKKf[h][:], Cf[h][:]), inc="TB")
                tb8[h] = lb.cnt["TB"]
                if ch + 2 < NCH:
                    lb.wait("sync", "TB", tb8[h])
                    lb.wait("scalar", "TB", tb8[h])
                    emit_loads(ch + 2, h)
            for h in range(NH):
                r = regions[h]
                lb.wait("vector", "TB", tb8[h])
                lb.op("vector", lambda v, h=h, ch=ch, r=r:
                      v.tensor_copy(oTp[h // 2][64 * (h % 2):64 * (h % 2) + DV,
                                    ch * P:(ch + 1) * P], r["O"]))
                lb.op("vector", lambda v, h=h, r=r:
                      v.tensor_add(Gf[h][:], Gf[h][:], r["S"][:, 2 * DK:3 * DK]))
                lb.op("vector", lambda v, h=h, r=r:
                      v.tensor_add(Gf[h][:], Gf[h][:], r["G"]))
                lb.op("vector", lambda v, h=h, r=r:
                      v.tensor_add(Cf[h][:], Cf[h][:], r["S"][:, DK:2 * DK]))
                lb.op("vector", lambda v, h=h:
                      v.tensor_add(Sf[h][:], Sf[h][:], bKKf[h][:]), dinc="VB")
                vb9[h] = lb.cnt["VB"]
                if ch < NCH - 1:
                    lb.wait("scalar", "VB", vb9[h])
                    lb.op("scalar", lambda s, h=h: s.copy(Sb[h][:], Sf[h][:]),
                          dinc="SB2")
                    sb10[h] = lb.cnt["SB2"]
        lb.wait("sync", "VB", lb.cnt["VB"])
        for h in range(NH):
            lb.op("sync", lambda s, h=h:
                  s.dma_start(oTh[h], oTp[h // 2][64 * (h % 2):64 * (h % 2) + DV, :]),
                  inc="OST", dma=True)
        lb.wait("sync", "OST", lb.cnt["OST"])
        lb.finish()
    return nc



def build_launch_w():
    nc = bass.Bass("TRN2", target_bir_lowering=False)
    w1 = nc.dram_tensor("w1", [D, FF], F32, kind="ExternalInput").ap()
    w2 = nc.dram_tensor("w2", [FF, D], F32, kind="ExternalInput").ap()
    w1o = nc.dram_tensor("w1o", [D, FF], BF16, kind="ExternalOutput").ap()
    w2o = nc.dram_tensor("w2o", [FF, D], BF16, kind="ExternalOutput").ap()

    tiles = [(w1[k * P:(k + 1) * P, :], w1o[k * P:(k + 1) * P, :], FF)
             for k in range(KD)]
    tiles += [(w2[f * P:(f + 1) * P, :], w2o[f * P:(f + 1) * P, :], D)
              for f in range(KF)]

    with ExitStack() as ctx:
        E = ctx.enter_context
        lb = LB(nc, ctx)
        nbuf = 4
        stg = [E(nc.sbuf_tensor(f"stg{i}", [P, FF], F32)) for i in range(nbuf)]
        out = [E(nc.sbuf_tensor(f"out{i}", [P, FF], BF16)) for i in range(nbuf)]
        for i, (srcp, dst, n) in enumerate(tiles):
            b = i % nbuf
            ld, cst, st = f"LD{b}", f"CS{b}", f"ST{b}"
            lb.wait("sync", st, lb.cnt[st])
            lb.op("sync", lambda s, srcp=srcp, b=b, n=n:
                  s.dma_start(stg[b][:, 0:n], srcp), inc=ld, dma=True)
            lb.wait("gpsimd", ld, lb.cnt[ld])
            lb.op("gpsimd", lambda g, b=b, n=n:
                  g.tensor_copy(out[b][:, 0:n], stg[b][:, 0:n]), dinc=cst)
            lb.wait("sync", cst, lb.cnt[cst])
            lb.op("sync", lambda s, dst=dst, b=b, n=n:
                  s.dma_start(dst, out[b][:, 0:n]), inc=st, dma=True)
        for b in range(nbuf):
            lb.wait("sync", f"ST{b}", lb.cnt[f"ST{b}"])
        lb.finish()
    return nc



def build_launch_c(ng=NG, ne=NE):
    span = ng * 512
    nc = bass.Bass("TRN2", target_bir_lowering=False)
    x = nc.dram_tensor("x", [span, D], F32, kind="ExternalInput").ap()
    oT = nc.dram_tensor("oT", [D, span], BF16, kind="ExternalInput").ap()
    Wo = nc.dram_tensor("Wo", [D, D], F32, kind="ExternalInput").ap()
    rw = nc.dram_tensor("rw", [D, NE], F32, kind="ExternalInput").ap()
    w1b = nc.dram_tensor("w1b", [NE, D, FF], BF16, kind="ExternalInput").ap()
    b1L = nc.dram_tensor("b1L", [NE, P, KF], F32, kind="ExternalInput").ap()
    w2b = nc.dram_tensor("w2b", [NE, FF, D], BF16, kind="ExternalInput").ap()
    b2r = nc.dram_tensor("b2r", [NE, 1, D], F32, kind="ExternalInput").ap()
    gmr = nc.dram_tensor("gmr", [NE, 1, D], F32, kind="ExternalInput").ap()
    btr = nc.dram_tensor("btr", [NE, 1, D], F32, kind="ExternalInput").ap()
    x2 = nc.dram_tensor("x2", [span, D], F32, kind="ExternalOutput").ap()
    xn2T = nc.dram_tensor("xn2T", [D, span], BF16, kind="ExternalOutput").ap()
    x1d = nc.dram_tensor("x1d", [span, D], F32, kind="Internal").ap()
    xn1fd = nc.dram_tensor("xn1fd", [span, D], F32, kind="Internal").ap()
    xn1Td = nc.dram_tensor("xn1Td", [D, span], BF16, kind="Internal").ap()
    yacc = nc.dram_tensor("yacc", [span, D], F32, kind="Internal").ap()

    with ExitStack() as ctx:
        E = ctx.enter_context
        lb = LB(nc, ctx)
        WB1 = E(nc.sbuf_tensor("WB1", [P, KD * FF], BF16))
        WB2 = E(nc.sbuf_tensor("WB2", [P, KF * D], BF16))
        HT = E(nc.sbuf_tensor("HT", [P, KF * 512], BF16))
        XT = E(nc.sbuf_tensor("XT", [P, KD * 512], BF16))
        XN = E(nc.sbuf_tensor("XN", [P, 4 * D], BF16))
        X1T = E(nc.sbuf_tensor("X1T", [P, D], F32))
        RES = E(nc.sbuf_tensor("RES", [P, D], F32))
        NRM = E(nc.sbuf_tensor("NRM", [P, D], F32))
        BC = E(nc.sbuf_tensor("BC", [P, 2 * D], F32))
        B2B = E(nc.sbuf_tensor("B2B", [1, D], BF16))
        ones1b = E(nc.sbuf_tensor("ones1b", [1, P], BF16))
        B1C = E(nc.sbuf_tensor("B1C", [P, KF], F32))
        GATES = E(nc.sbuf_tensor("GATES", [P, ng * 4 * NE], F32))
        SHR = E(nc.sbuf_tensor("SHR", [P, D], F32))
        GR = SHR[:][0:1, :]
        RWF = SHR[:][:, 0:KD * NE]
        _shb = SHR[:].bitcast(BF16)
        RWHI = _shb[:, 2 * KD * NE:3 * KD * NE]
        RWLO = _shb[:, 3 * KD * NE:4 * KD * NE]
        T8 = [SHR[:][:, 2 * KD * NE + 8 * i:2 * KD * NE + 8 * (i + 1)]
              for i in range(3)]
        IOT = SHR[:][:, 2 * KD * NE + 24:2 * KD * NE + 24 + NE]
        ident = _shb[:, 320:448]
        ones1 = E(nc.sbuf_tensor("ones1", [1, P], F32))
        cols = ([E(nc.sbuf_tensor(f"col{i}", [P, 1], F32)) for i in range(5)]
                + [SHR[:][:, 224 + i:225 + i] for i in range(4)])
        epsc = E(nc.sbuf_tensor("epsc", [P, 1], F32))
        epsm = E(nc.sbuf_tensor("epsm", [P, 1], F32))
        STG = HT[:].bitcast(F32)[:, 0:D]
        LOG = HT[:][:, 2 * D:6 * D]
        LOGT = HT[:][:, 6 * D:10 * D]
        ps_l1 = [E(nc.psum_tensor(f"ps_l1{i}", [P, 512], F32)) for i in range(2)]
        ps_l2 = [E(nc.psum_tensor(f"ps_l2{i}", [P, 512], F32)) for i in range(4)]
        ps_tr = [E(nc.psum_tensor(f"ps_tr{i}", [P, 4 * P], BF16)) for i in range(2)]
        ps_bc = ps_l1[0]
        wk = {"sq": NRM[:], "c0": cols[0], "c1": cols[1], "c2": cols[2],
              "c3": cols[3], "c4": cols[4]}

        make_ident(lb, ident)
        const_col(lb, epsc, LN_EPS)
        const_col(lb, epsm, MOE_EPS)
        lb.op("gpsimd", lambda g: g.memset(ones1[:], 1.0), dinc="CONST")
        lb.op("gpsimd", lambda g: g.memset(ones1b[:], 1.0), dinc="CONST")
        lb.op("gpsimd", lambda g: g.iota(IOT[:], pattern=[[1, NE]], base=0,
                                         channel_multiplier=0,
                                         allow_small_or_imprecise_dtypes=True),
              dinc="CONST")
        lb.wait("scalar", "CONST", lb.cnt["CONST"])
        lb.wait("vector", "CONST", lb.cnt["CONST"])

        wpairs = [(Wo[k * P:(k + 1) * P, :], WB1[:, k * FF:k * FF + D]) for k in range(KD)]
        wpairs += [(rw[k * P:(k + 1) * P, :], RWF[:, k * NE:(k + 1) * NE]) for k in range(KD)]
        stage_weights(lb, wpairs, STG)
        lb.wait("vector", "WCST", lb.cnt["WCST"])
        lb.op("vector", lambda v: v.tensor_copy(RWHI[:], RWF[:]))
        lb.op("vector", lambda v: v.tensor_sub(RWLO[:], RWF[:], RWHI[:]), dinc="RWS")
        lb.wait("tensor", "RWS", lb.cnt["RWS"])
        lb.wait("tensor", "WCST", lb.cnt["WCST"])
        lb.wait("tensor", "CONST", lb.cnt["CONST"])

        atct = 0
        for g in range(ng):
            if g > 0:
                lb.wait("sync", "TRP", 8 * g)
            for k in range(KD):
                lb.op("sync", lambda s, g=g, k=k:
                      s.dma_start(WB2[:, k * 512:(k + 1) * 512],
                                  oT[k * P:(k + 1) * P, g * 512:(g + 1) * 512]),
                      inc="XLD", dma=True)
            lb.wait("tensor", "XLD", lb.cnt["XLD"])
            if g > 0:
                lb.wait("vector", "TRP", 8 * g)
            for tt in range(4):
                lb.wait("sync", "LNV", lb.cnt["LNV"])
                lb.wait("sync", "C1ST", lb.cnt["C1ST"])
                lb.op("sync", lambda s, g=g, tt=tt:
                      s.dma_start(X1T[:],
                                  x[g * 512 + tt * P:g * 512 + (tt + 1) * P, :]),
                      inc="XLT", dma=True)
                lb.wait("vector", "XLT", lb.cnt["XLT"])
                if atct >= 1:
                    lb.wait("tensor", "PEV", atct)
                for db in range(2):
                    for k in range(KD):
                        lb.op("tensor", lambda t, k=k, tt=tt, db=db:
                              t.matmul(ps_l2[db][:],
                                       WB2[:, k * 512 + tt * P:k * 512 + (tt + 1) * P],
                                       WB1[:, k * FF + db * 512:k * FF + (db + 1) * 512],
                                       start=(k == 0), stop=(k == KD - 1)),
                              inc="PMM" if k == KD - 1 else None)
                lb.wait("vector", "PMM", lb.cnt["PMM"])
                lb.wait("vector", "XNST", lb.cnt["XNST"])
                lb.op("vector", lambda v: v.tensor_copy(RES[:, 0:512], ps_l2[0][:]))
                lb.op("vector", lambda v: v.tensor_copy(RES[:, 512:D], ps_l2[1][:]),
                      dinc="PEV")
                atct += 1
                emit_ln(lb, RES[:], NRM[:], wk, epsc[:], D)
                lb.op("vector", lambda v:
                      v.tensor_add(X1T[:], X1T[:], NRM[:]), dinc="LNV")
                lb.wait("sync", "LNV", lb.cnt["LNV"])
                lb.op("sync", lambda s, g=g, tt=tt:
                      s.dma_start(x1d[g * 512 + tt * P:g * 512 + (tt + 1) * P, :],
                                  X1T[:]), inc="C1ST", dma=True)
                emit_ln(lb, X1T[:], XN[:, tt * D:(tt + 1) * D], wk, epsc[:], D,
                        out_raw=RES[:])
                lb.op("vector", lambda v, tt=tt:
                      v.tensor_sub(LOG[:, tt * D:(tt + 1) * D], RES[:],
                                   XN[:, tt * D:(tt + 1) * D]), dinc="LNV")
                lb.wait("sync", "LNV", lb.cnt["LNV"])
                lb.op("sync", lambda s, g=g, tt=tt:
                      s.dma_start(xn1fd[g * 512 + tt * P:g * 512 + (tt + 1) * P, :],
                                  RES[:]), inc="XNST", dma=True)
            c1_lnv = lb.cnt["LNV"]
            lb.wait("vector", "C1ST", lb.cnt["C1ST"])
            lb.wait("tensor", "LNV", c1_lnv)
            emit_transposes(lb, lambda k, tt: XN[:, tt * D + k * P:tt * D + (k + 1) * P],
                            lambda k, tt: XT[:, k * 512 + tt * P:k * 512 + (tt + 1) * P],
                            ident, ps_tr, 4)
            lb.wait("sync", "TRE", lb.cnt["TRE"])
            for k in range(KD):
                lb.op("sync", lambda s, g=g, k=k:
                      s.dma_start(xn1Td[k * P:(k + 1) * P, g * 512:(g + 1) * 512],
                                  XT[:, k * 512:(k + 1) * 512]), inc="C1ST", dma=True)
            lb.wait("scalar", "LNV", c1_lnv)
            if g > 0:
                lb.wait("scalar", "RMM", lb.cnt["RMM"])
            for tt in range(4):
                for k in range(KD):
                    lb.op("scalar", lambda ss, k=k, tt=tt:
                          ss.dma_start_transpose(
                              LOGT[:, k * 512 + tt * P:k * 512 + (tt + 1) * P],
                              LOG[:, tt * D + k * P:tt * D + (k + 1) * P]),
                          inc="XTR", dma=True)
            lb.wait("tensor", "TRE", lb.cnt["TRE"])
            lb.wait("tensor", "XTR", lb.cnt["XTR"])
            if g > 0:
                lb.wait("tensor", "RCP", 4 * g)
            for tt in range(4):
                nk = 0
                for k in range(KD):
                    for (lhs_buf, rhs_buf) in ((XT, RWHI), (XT, RWLO), (LOGT, RWHI)):
                        nk += 1
                        lb.op("tensor",
                              lambda t, k=k, tt=tt, lhs_buf=lhs_buf, rhs_buf=rhs_buf,
                              nk=nk:
                              t.matmul(ps_bc[:, tt * NE:(tt + 1) * NE],
                                       lhs_buf[:, k * 512 + tt * P:k * 512 + (tt + 1) * P],
                                       rhs_buf[:, k * NE:(k + 1) * NE],
                                       start=(nk == 1), stop=(nk == 3 * KD)),
                              inc="RMM" if nk == 3 * KD else None)
            lb.wait("vector", "RMM", lb.cnt["RMM"])
            for tt in range(4):
                off = (g * 4 + tt) * NE
                lgt = T8[0]
                lb.op("vector", lambda v, tt=tt, lgt=lgt:
                      v.tensor_copy(lgt[:], ps_bc[:, tt * NE:(tt + 1) * NE]), dinc="RCP")
                lb.op("vector", lambda v, lgt=lgt: v.reduce_max(cols[5][:], lgt[:], AX.X))
                lb.op("vector", lambda v, lgt=lgt:
                      v.tensor_scalar(T8[1][:], lgt[:], cols[5][:], None, ALU.is_equal))
                lb.op("vector", lambda v:
                      v.tensor_scalar(T8[1][:], T8[1][:], -1e9, 1e9, ALU.mult, ALU.add))
                lb.op("vector", lambda v:
                      v.scalar_tensor_tensor(T8[1][:], IOT[:], 1.0, T8[1][:],
                                             ALU.mult, ALU.add))
                lb.op("vector", lambda v:
                      v.tensor_reduce(cols[6][:], T8[1][:], AX.X, ALU.min))
                lb.op("vector", lambda v:
                      v.tensor_scalar(T8[1][:], IOT[:], cols[6][:], None, ALU.is_equal))
                lb.op("vector", lambda v, lgt=lgt:
                      v.scalar_tensor_tensor(T8[2][:], T8[1][:], -1e9, lgt[:],
                                             ALU.mult, ALU.add))
                lb.op("vector", lambda v: v.reduce_max(cols[6][:], T8[2][:], AX.X))
                lb.op("vector", lambda v:
                      v.tensor_scalar(T8[2][:], T8[2][:], cols[6][:], None, ALU.is_equal))
                lb.op("vector", lambda v:
                      v.tensor_scalar(T8[2][:], T8[2][:], -1e9, 1e9, ALU.mult, ALU.add))
                lb.op("vector", lambda v:
                      v.scalar_tensor_tensor(T8[2][:], IOT[:], 1.0, T8[2][:],
                                             ALU.mult, ALU.add))
                lb.op("vector", lambda v:
                      v.tensor_reduce(cols[7][:], T8[2][:], AX.X, ALU.min))
                lb.op("vector", lambda v:
                      v.tensor_scalar(T8[2][:], IOT[:], cols[7][:], None, ALU.is_equal))
                lb.op("vector", lambda v: v.tensor_sub(cols[7][:], cols[6][:], cols[5][:]),
                      dinc="GTV")
                lb.wait("scalar", "GTV", lb.cnt["GTV"])
                lb.op("scalar", lambda s: s.activation(cols[7][:], cols[7][:], AF.Exp),
                      dinc="GTS")
                lb.wait("vector", "GTS", lb.cnt["GTS"])
                lb.op("vector", lambda v:
                      v.tensor_scalar(cols[8][:], cols[7][:], 1.0, None, ALU.add))
                lb.op("vector", lambda v: v.reciprocal(cols[8][:], cols[8][:]))
                lb.op("vector", lambda v:
                      v.tensor_scalar(cols[7][:], cols[8][:], -1.0, 1.0, ALU.mult, ALU.add))
                lb.op("vector", lambda v:
                      v.tensor_scalar(T8[1][:], T8[1][:], cols[8][:], None, ALU.mult))
                lb.op("vector", lambda v, off=off:
                      v.scalar_tensor_tensor(GATES[:, off:off + NE], T8[2][:], cols[7][:],
                                             T8[1][:], ALU.mult, ALU.add), dinc="GTV")

        lb.wait("sync", "C1ST", lb.cnt["C1ST"])
        lb.wait("sync", "XNST", lb.cnt["XNST"])
        for eng in ENGS:
            lb.op(eng, lambda e: e.nop(), inc="PH1")
        for eng in ENGS:
            lb.wait(eng, "PH1", 5)

        iters = [(e, g) for e in range(ne) for g in range(ng)]
        XTB = [XT, XN]
        xtld = [0, 0]

        def issue_xt(t):
            b = t % 2
            if t >= 2:
                lb.wait("sync", "L1MM", 32 * (t - 1))
            gt = iters[t][1]
            for k in range(KD):
                lb.op("sync", lambda s, gt=gt, k=k, b=b:
                      s.dma_start(XTB[b][:, k * 512:(k + 1) * 512],
                                  xn1Td[k * P:(k + 1) * P, gt * 512:(gt + 1) * 512]),
                      inc=f"XL2_{b}", dma=True)
            xtld[b] += 1

        issue_xt(0)
        issue_xt(1)
        l2ct = 0
        for idx, (e, g) in enumerate(iters):
            bX = idx % 2
            XTc = XTB[bX]
            if g == 0:
                lb.wait("sync", "L1MM", 32 * 8 * e)
                for k in range(KD):
                    lb.op("sync", lambda s, e=e, k=k:
                          s.dma_start(WB1[:, k * FF:(k + 1) * FF],
                                      w1b[e, k * P:(k + 1) * P, :]),
                          inc="WB1L", dma=True)
                lb.wait("sync", "L1EV", 256 * e)
                lb.op("sync", lambda s, e=e: s.dma_start(B1C[:], b1L[e]),
                      inc="B1CL", dma=True)
                lb.wait("scalar", "L2MM", 64 * e)
                for f in range(KF):
                    lb.op("scalar", lambda s, e=e, f=f:
                          s.dma_start(WB2[:, f * D:(f + 1) * D],
                                      w2b[e, f * P:(f + 1) * P, :]),
                          inc="WB2L", dma=True)
                lb.wait("tensor", "L1EV", 256 * e)
                for i, src in enumerate((gmr, btr, b2r)):
                    lb.wait("sync", "BCM", 4 * e + min(2 * i, 4))
                    if i == 0:
                        lb.wait("sync", "B2C", e)
                    lb.op("sync", lambda s, e=e, src=src:
                          s.dma_start(GR[:], src[e]), inc="ROWL", dma=True)
                for j in range(4):
                    so = (j % 2) * 512
                    do = (j // 2) * D + so
                    lb.wait("tensor", "ROWL", 48 * e + 16 * (j // 2 + 1))
                    if j >= 2:
                        lb.wait("tensor", "BCV", 4 * e + j - 1)
                    lb.op("tensor", lambda t, j=j, so=so:
                          t.matmul(ps_l1[j % 2][:], ones1[:], GR[:, so:so + 512]),
                          inc="BCM")
                    lb.wait("vector", "BCM", 4 * e + j + 1)
                    lb.op("vector", lambda v, j=j, do=do:
                          v.tensor_copy(BC[:, do:do + 512], ps_l1[j % 2][:]),
                          dinc="BCV")
                lb.wait("scalar", "ROWL", 48 * (e + 1))
                lb.op("scalar", lambda s: s.copy(B2B[:], GR[:]), dinc="B2C")
                lb.wait("scalar", "B1CL", 16 * (e + 1))
                lb.wait("tensor", "WB1L", 16 * 8 * (e + 1))
                lb.wait("tensor", "BCV", 4 * (e + 1))
            lb.wait("tensor", f"XL2_{bX}", 16 * 8 * (idx // 2 + 1))
            for f in range(KF):
                pb = f % 2
                if lb.cnt["L1MM"] >= 2:
                    lb.wait("tensor", "L1EV", lb.cnt["L1MM"] - 1)
                for k in range(KD):
                    lb.op("tensor", lambda t, f=f, k=k, pb=pb, XTc=XTc:
                          t.matmul(ps_l1[pb][:],
                                   WB1[:, k * FF + f * P:k * FF + (f + 1) * P],
                                   XTc[:, k * 512:(k + 1) * 512],
                                   start=(k == 0), stop=(k == KD - 1)),
                          inc="L1MM" if k == KD - 1 else None)
                lb.wait("scalar", "L1MM", lb.cnt["L1MM"])
                lb.op("scalar", lambda s, f=f, pb=pb:
                      s.activation(HT[:, f * 512:(f + 1) * 512], ps_l1[pb][:],
                                   AF.Gelu, bias=B1C[:, f:f + 1], scale=1.0),
                      dinc="L1EV")
            if idx + 2 < len(iters):
                issue_xt(idx + 2)
            lb.wait("tensor", "L1EV", lb.cnt["L1EV"])
            if g == 0:
                lb.wait("tensor", "WB2L", 16 * KF * (e + 1))
                lb.wait("tensor", "B2C", e + 1)
            for tt in range(4):
                lb.wait("sync", "L2EV", lb.cnt["L2EV"])
                lb.op("sync", lambda s, g=g, tt=tt:
                      s.dma_start(X1T[:],
                                  xn1fd[g * 512 + tt * P:g * 512 + (tt + 1) * P, :]),
                      inc="XL2T", dma=True)
                for db in range(2):
                    bank = l2ct % 4
                    l2ct += 1
                    if l2ct > 4:
                        lb.wait("tensor", "L2EV", l2ct - 4)
                    for f in range(KF):
                        lb.op("tensor", lambda t, f=f, tt=tt, db=db, bank=bank:
                              t.matmul(ps_l2[bank][:],
                                       HT[:, f * 512 + tt * P:f * 512 + (tt + 1) * P],
                                       WB2[:, f * D + db * 512:f * D + (db + 1) * 512],
                                       start=(f == 0), stop=False))
                    lb.op("tensor", lambda t, db=db, bank=bank:
                          t.matmul(ps_l2[bank][:], ones1b[:],
                                   B2B[:, db * 512:(db + 1) * 512],
                                   start=False, stop=True), inc="L2MM")
                lb.wait("vector", "L2MM", lb.cnt["L2MM"])
                lb.wait("vector", "XL2T", lb.cnt["XL2T"])
                bank0, bank1 = (l2ct - 2) % 4, (l2ct - 1) % 4
                lb.op("vector", lambda v, bank0=bank0:
                      v.tensor_add(RES[:, 0:512], ps_l2[bank0][:], X1T[:, 0:512]),
                      dinc="L2EV")
                lb.op("vector", lambda v, bank1=bank1:
                      v.tensor_add(RES[:, 512:D], ps_l2[bank1][:], X1T[:, 512:D]),
                      dinc="L2EV")
                lb.wait("vector", "YST", lb.cnt["YST"])
                emit_ln(lb, RES[:], NRM[:], wk, epsm[:], D)
                off = (g * 4 + tt) * NE
                lb.op("vector", lambda v:
                      v.tensor_mul(NRM[:], NRM[:], BC[:, 0:D]))
                lb.op("vector", lambda v:
                      v.tensor_add(NRM[:], NRM[:], BC[:, D:2 * D]))
                lb.op("vector", lambda v, off=off, e=e:
                      v.tensor_scalar(NRM[:], NRM[:], GATES[:, off + e:off + e + 1],
                                      None, ALU.mult), dinc="YEV")
                lb.wait("gpsimd", "YEV", lb.cnt["YEV"])
                if e == 0:
                    lb.op("gpsimd", lambda gg, g=g, tt=tt:
                          gg.dma_start(yacc[g * 512 + tt * P:g * 512 + (tt + 1) * P, :],
                                       NRM[:]), inc="YST", dma=True)
                else:
                    lb.op("gpsimd", lambda gg, g=g, tt=tt:
                          gg.dma_start(yacc[g * 512 + tt * P:g * 512 + (tt + 1) * P, :],
                                       NRM[:], accum_op=ALU.add), inc="YST", dma=True)

        lb.wait("sync", "YST", lb.cnt["YST"])
        for eng in ENGS:
            lb.op(eng, lambda e: e.nop(), inc="PH2")
        for eng in ENGS:
            lb.wait(eng, "PH2", 5)
        make_ident(lb, ident)
        lb.wait("tensor", "CONST", lb.cnt["CONST"])
        for g in range(ng):
            for tt in range(4):
                lnv_pre = lb.cnt["LNV"]
                lb.wait("sync", "LNV", lnv_pre)
                lb.wait("sync", "C3ST", lb.cnt["C3ST"])
                lb.op("sync", lambda s, g=g, tt=tt:
                      s.dma_start(X1T[:],
                                  x1d[g * 512 + tt * P:g * 512 + (tt + 1) * P, :]),
                      inc="XLD3", dma=True)
                lb.op("sync", lambda s, g=g, tt=tt:
                      s.dma_start(RES[:], yacc[g * 512 + tt * P:g * 512 + (tt + 1) * P, :]),
                      inc="XLD3", dma=True)
                lb.wait("vector", "XLD3", lb.cnt["XLD3"])
                if g > 0:
                    lb.wait("vector", "TRP", lb.cnt["TRP"])
                emit_ln(lb, RES[:], NRM[:], wk, epsc[:], D)
                lb.op("vector", lambda v:
                      v.tensor_add(X1T[:], X1T[:], NRM[:]), dinc="LNV")
                lb.wait("sync", "LNV", lb.cnt["LNV"])
                lb.op("sync", lambda s, g=g, tt=tt:
                      s.dma_start(x2[g * 512 + tt * P:g * 512 + (tt + 1) * P, :],
                                  X1T[:]), inc="C3ST", dma=True)
                emit_ln(lb, X1T[:], XN[:, tt * D:(tt + 1) * D], wk, epsc[:], D)
            c3_lnv = lb.cnt["LNV"]
            lb.wait("tensor", "LNV", c3_lnv)
            lb.wait("vector", "C3ST", lb.cnt["C3ST"])
            emit_transposes(lb, lambda k, tt: XN[:, tt * D + k * P:tt * D + (k + 1) * P],
                            lambda k, tt: XT[:, k * 512 + tt * P:k * 512 + (tt + 1) * P],
                            ident, ps_tr, 4)
            lb.wait("sync", "TRE", lb.cnt["TRE"])
            for k in range(KD):
                lb.op("sync", lambda s, g=g, k=k:
                      s.dma_start(xn2T[k * P:(k + 1) * P, g * 512:(g + 1) * 512],
                                  XT[:, k * 512:(k + 1) * 512]), inc="C3ST", dma=True)
        lb.wait("sync", "C3ST", lb.cnt["C3ST"])
        lb.finish()
    return nc



def build_launch_d(T=T_, nb=B_):
    HB = T // 2
    NGD = HB // 512
    nc = bass.Bass("TRN2", target_bir_lowering=False)
    xn2T = nc.dram_tensor("xn2T", [D, nb * T], BF16, kind="ExternalInput").ap()
    WinT = nc.dram_tensor("WinT", [D, P], F32, kind="ExternalInput").ap()
    WigT = nc.dram_tensor("WigT", [D, P], F32, kind="ExternalInput").ap()
    WrgT = nc.dram_tensor("WrgT", [D, P], F32, kind="ExternalInput").ap()
    big = nc.dram_tensor("big", [P, 1], F32, kind="ExternalInput").ap()
    brg = nc.dram_tensor("brg", [P, 1], F32, kind="ExternalInput").ap()
    lam = nc.dram_tensor("lam", [P, 1], F32, kind="ExternalInput").ap()
    h0 = nc.dram_tensor("h0", [P, 1], F32, kind="ExternalInput").ap()
    hout = nc.dram_tensor("hout", [nb, P, T], F32, kind="ExternalOutput").ap()

    with ExitStack() as ctx:
        E = ctx.enter_context
        lb = LB(nc, ctx)
        WB = E(nc.sbuf_tensor("WB", [P, 3 * KD * P], BF16))
        XTD = E(nc.sbuf_tensor("XTD", [P, KD * 512], BF16))
        A = E(nc.sbuf_tensor("A", [P, HB], F32))
        G = E(nc.sbuf_tensor("G", [P, HB], F32))
        HH = E(nc.sbuf_tensor("HH", [P, HB], F32))
        XP = E(nc.sbuf_tensor("XP", [P, 512], F32))
        I_ = E(nc.sbuf_tensor("I_", [P, 512], F32))
        SQ2 = E(nc.sbuf_tensor("SQ2", [P, 512], F32))
        bigc = E(nc.sbuf_tensor("bigc", [P, 1], F32))
        brgc = E(nc.sbuf_tensor("brgc", [P, 1], F32))
        lamc = E(nc.sbuf_tensor("lamc", [P, 1], F32))
        h0c = E(nc.sbuf_tensor("h0c", [P, 1], F32))
        la8 = E(nc.sbuf_tensor("la8", [P, 1], F32))
        onepc = E(nc.sbuf_tensor("onepc", [P, 1], F32))
        eps8c = E(nc.sbuf_tensor("eps8c", [P, 1], F32))
        hin = E(nc.sbuf_tensor("hin", [P, 1], F32))
        STGD = E(nc.sbuf_tensor("STGD", [P, P], F32))
        ps = [E(nc.psum_tensor(f"psd{i}", [P, 512], F32)) for i in range(6)]

        const_col(lb, onepc, 1.0 + 1e-6)
        const_col(lb, eps8c, 1e-8)
        lb.wait("scalar", "CONST", lb.cnt["CONST"])

        wpairs = []
        for j, W in enumerate((WinT, WigT, WrgT)):
            for k in range(KD):
                wpairs.append((W[k * P:(k + 1) * P, :],
                               WB[:, (j * KD + k) * P:(j * KD + k + 1) * P]))
        stage_weights(lb, wpairs, STGD)
        for src, dst in ((big, bigc), (brg, brgc), (lam, lamc), (h0, h0c)):
            lb.op("sync", lambda s, src=src, dst=dst: s.dma_start(dst[:], src),
                  inc="VLD", dma=True)
        lb.wait("scalar", "VLD", lb.cnt["VLD"])
        lb.op("scalar", lambda s: s.activation(la8[:], lamc[:], AF.Sigmoid), dinc="LA")
        lb.op("scalar", lambda s: s.activation(la8[:], la8[:], AF.Ln, bias=eps8c[:]),
              dinc="LA")
        lb.wait("vector", "LA", lb.cnt["LA"])
        lb.op("vector", lambda v: v.tensor_scalar(la8[:], la8[:], 8.0, None, ALU.mult),
              dinc="LA")
        lb.wait("scalar", "LA", lb.cnt["LA"])
        lb.wait("tensor", "WCST", lb.cnt["WCST"])

        pct = 0
        for hb in range(2 * nb):
            b = hb // 2
            base = b * T + (hb % 2) * HB
            for gg in range(NGD):
                off = gg * 512
                lb.wait("sync", "PRMM", lb.cnt["PRMM"])
                for k in range(KD):
                    lb.op("sync", lambda s, k=k, base=base, off=off:
                          s.dma_start(XTD[:, k * 512:(k + 1) * 512],
                                      xn2T[k * P:(k + 1) * P, base + off:base + off + 512]),
                          inc="XLDD", dma=True)
                lb.wait("tensor", "XLDD", lb.cnt["XLDD"])
                pset = (pct % 2) * 3
                pct += 1
                if pct >= 3:
                    lb.wait("tensor", "PEVD", pct - 2)
                for j in range(3):
                    for k in range(KD):
                        lb.op("tensor", lambda t, j=j, k=k, pset=pset:
                              t.matmul(ps[pset + j][:],
                                       WB[:, (j * KD + k) * P:(j * KD + k + 1) * P],
                                       XTD[:, k * 512:(k + 1) * 512],
                                       start=(k == 0), stop=(k == KD - 1)),
                              inc="PRMM" if k == KD - 1 else None)
                lb.wait("vector", "PRMM", lb.cnt["PRMM"])
                lb.wait("scalar", "PRMM", lb.cnt["PRMM"])
                lb.op("vector", lambda v, pset=pset: v.tensor_copy(XP[:], ps[pset][:]),
                      dinc="ELV")
                lb.op("scalar", lambda s, pset=pset:
                      s.activation(I_[:], ps[pset + 1][:], AF.Sigmoid, bias=bigc[:]))
                lb.op("scalar", lambda s, pset=pset:
                      s.activation(SQ2[:], ps[pset + 2][:], AF.Sigmoid, bias=brgc[:]))
                lb.op("scalar", lambda s, off=off:
                      s.activation(A[:, off:off + 512], SQ2[:], AF.Exp, scale=la8[:]),
                      dinc="ELS")
                lb.op("scalar", lambda s, off=off:
                      s.activation(SQ2[:], A[:, off:off + 512], AF.Square))
                lb.op("scalar", lambda s:
                      s.activation(SQ2[:], SQ2[:], AF.Sqrt, bias=onepc[:], scale=-1.0),
                      dinc="ELS")
                lb.wait("scalar", "ELV", lb.cnt["ELV"])
                lb.op("scalar", lambda s: s.nop(), dinc="PEVD")
                lb.wait("vector", "ELS", lb.cnt["ELS"])
                lb.op("vector", lambda v: v.tensor_mul(SQ2[:], SQ2[:], I_[:]))
                lb.op("vector", lambda v, off=off:
                      v.tensor_mul(G[:, off:off + 512], SQ2[:], XP[:]), dinc="ELV")
            if hb % 2 == 0:
                lb.op("vector", lambda v: v.tensor_copy(hin[:], h0c[:]))
            lb.wait("vector", "ELS", lb.cnt["ELS"])
            lb.op("vector", lambda v:
                  v.tensor_tensor_scan(HH[:], A[:], G[:], hin[:, 0:1],
                                       ALU.mult, ALU.add), dinc="SCN")
            lb.op("vector", lambda v: v.tensor_copy(hin[:], HH[:, HB - 1:HB]),
                  dinc="SCN")
            lb.wait("sync", "SCN", lb.cnt["SCN"])
            lb.op("sync", lambda s, b=b, hb=hb:
                  s.dma_start(hout[b, :, (hb % 2) * HB:(hb % 2) * HB + HB], HH[:]),
                  inc="HST", dma=True)
            lb.wait("scalar", "SCN", lb.cnt["SCN"])
            lb.wait("vector", "HST", lb.cnt["HST"])
        lb.wait("sync", "HST", lb.cnt["HST"])
        lb.finish()
    return nc



def build_launch_f(ng=NG):
    span = ng * 512
    nc = bass.Bass("TRN2", target_bir_lowering=False)
    x2 = nc.dram_tensor("x2", [span, D], F32, kind="ExternalInput").ap()
    hT = nc.dram_tensor("hT", [D, span], F32, kind="ExternalInput").ap()
    nwL = nc.dram_tensor("nwL", [P, KD], F32, kind="ExternalInput").ap()
    WoutT = nc.dram_tensor("WoutT", [D, D], F32, kind="ExternalInput").ap()
    Win = nc.dram_tensor("Win", [D, FM], F32, kind="ExternalInput").ap()
    binc_d = nc.dram_tensor("binc_d", [FM, 1], F32, kind="ExternalInput").ap()
    Ew1 = nc.dram_tensor("Ew1", [FM, D], F32, kind="ExternalInput").ap()
    b1fL = nc.dram_tensor("b1fL", [P, KD], F32, kind="ExternalInput").ap()
    W2blk = nc.dram_tensor("W2blk", [D, FM], F32, kind="ExternalInput").ap()
    b2c_d = nc.dram_tensor("b2c_d", [FM, 1], F32, kind="ExternalInput").ap()
    Wout1 = nc.dram_tensor("Wout1", [FM, D], F32, kind="ExternalInput").ap()
    boutr = nc.dram_tensor("boutr", [1, D], F32, kind="ExternalInput").ap()
    out = nc.dram_tensor("out", [span, D], F32, kind="ExternalOutput").ap()

    with ExitStack() as ctx:
        E = ctx.enter_context
        lb = LB(nc, ctx)
        HS = E(nc.sbuf_tensor("HS", [P, KD * 512], F32))
        HV = E(nc.sbuf_tensor("HV", [P, KD * 512], BF16))
        WoutTb = E(nc.sbuf_tensor("WoutTb", [P, KD * D], BF16))
        Winb = E(nc.sbuf_tensor("Winb", [P, KD * FM], BF16))
        Ew1b = E(nc.sbuf_tensor("Ew1b", [FM, D], BF16))
        W2blkb = E(nc.sbuf_tensor("W2blkb", [P, KD * FM], BF16))
        Wout1b = E(nc.sbuf_tensor("Wout1b", [FM, D], BF16))
        boutb = E(nc.sbuf_tensor("boutb", [1, D], BF16))
        X2T = E(nc.sbuf_tensor("X2T", [P, D], F32))
        X3 = E(nc.sbuf_tensor("X3", [P, D], F32))
        RES = E(nc.sbuf_tensor("RES", [P, D], F32))
        RES4 = E(nc.sbuf_tensor("RES4", [P, 3 * D], F32))
        NRM = E(nc.sbuf_tensor("NRM", [P, D], F32))
        SQ = E(nc.sbuf_tensor("SQ", [P, D], F32))
        SQF = E(nc.sbuf_tensor("SQF", [P, KD * 512], F32))
        XN3 = E(nc.sbuf_tensor("XN3", [P, D], BF16))
        XN3T = E(nc.sbuf_tensor("XN3T", [P, D], BF16))
        sTb = E(nc.sbuf_tensor("sTb", [FM, P], BF16))
        hTb = E(nc.sbuf_tensor("hTb", [P, D], BF16))
        gTb = E(nc.sbuf_tensor("gTb", [FM, P], BF16))
        NWC = E(nc.sbuf_tensor("NWC", [P, KD], F32))
        B1F = E(nc.sbuf_tensor("B1F", [P, KD], F32))
        binc = E(nc.sbuf_tensor("binc", [FM, 1], F32))
        b2c = E(nc.sbuf_tensor("b2c", [FM, 1], F32))
        rstds = E(nc.sbuf_tensor("rstds", [P, 4], F32))
        ident = E(nc.sbuf_tensor("ident", [P, P], BF16))
        ones1b = E(nc.sbuf_tensor("ones1b", [1, P], BF16))
        onescol = E(nc.sbuf_tensor("onescol", [P, 1], F32))
        cols = [E(nc.sbuf_tensor(f"col{i}", [P, 1], F32)) for i in range(5)]
        epsc = E(nc.sbuf_tensor("epsc", [P, 1], F32))
        epsr = E(nc.sbuf_tensor("epsr", [P, 1], F32))
        STG = E(nc.sbuf_tensor("STG", [P, D], F32))
        ps_l2 = [E(nc.psum_tensor(f"ps_l2{i}", [P, 512], F32)) for i in range(2)]
        ps_tr = [E(nc.psum_tensor(f"ps_tr{i}", [P, 4 * P], BF16)) for i in range(2)]
        ps_ss = E(nc.psum_tensor("ps_ss", [P, 4], F32))
        ps_s1 = E(nc.psum_tensor("ps_s1", [FM, P], F32))
        ps_h1 = [E(nc.psum_tensor(f"ps_h1{i}", [P, P], F32)) for i in range(2)]
        wk = {"sq": SQ, "c0": cols[0], "c1": cols[1], "c2": cols[2],
              "c3": cols[3], "c4": cols[4]}

        make_ident(lb, ident)
        const_col(lb, epsc, LN_EPS)
        const_col(lb, epsr, RMS_EPS)
        lb.op("gpsimd", lambda g: g.memset(ones1b[:], 1.0), dinc="CONST")
        lb.op("gpsimd", lambda g: g.memset(onescol[:], 1.0), dinc="CONST")
        lb.wait("scalar", "CONST", lb.cnt["CONST"])
        lb.wait("vector", "CONST", lb.cnt["CONST"])

        wpairs = [(WoutT[k * P:(k + 1) * P, :], WoutTb[:, k * D:(k + 1) * D])
                  for k in range(KD)]
        wpairs += [(Win[k * P:(k + 1) * P, :], Winb[:, k * FM:(k + 1) * FM])
                   for k in range(KD)]
        wpairs += [(Ew1[:, :], Ew1b[:, :])]
        wpairs += [(W2blk[k * P:(k + 1) * P, :], W2blkb[:, k * FM:(k + 1) * FM])
                   for k in range(KD)]
        wpairs += [(Wout1[:, :], Wout1b[:, :])]
        wpairs += [(boutr[:, :], boutb[:, :])]
        stage_weights(lb, wpairs, STG)
        for src, dst in ((nwL, NWC), (b1fL, B1F), (binc_d, binc), (b2c_d, b2c)):
            lb.op("sync", lambda s, src=src, dst=dst: s.dma_start(dst[:], src),
                  inc="VLD", dma=True)
        lb.wait("tensor", "WCST", lb.cnt["WCST"])
        lb.wait("tensor", "CONST", lb.cnt["CONST"])
        lb.wait("vector", "VLD", lb.cnt["VLD"])
        lb.wait("scalar", "VLD", lb.cnt["VLD"])

        l2ct = 0
        for g in range(ng):
            lb.wait("sync", "HVC", lb.cnt["HVC"])
            lb.wait("sync", "SQC", lb.cnt["SQC"])
            lb.wait("vector", "SSK", lb.cnt["SSK"])
            for k in range(KD):
                lb.op("sync", lambda s, g=g, k=k:
                      s.dma_start(HS[:, k * 512:(k + 1) * 512],
                                  hT[k * P:(k + 1) * P, g * 512:(g + 1) * 512]),
                      inc="HLD", dma=True)
            lb.wait("vector", "HLD", lb.cnt["HLD"])
            lb.wait("vector", "WOMM", lb.cnt["WOMM"])
            lb.wait("tensor", "RSV", lb.cnt["RSV"])
            for k in range(KD):
                lb.op("vector", lambda v, k=k:
                      v.tensor_mul(SQF[:, k * 512:(k + 1) * 512],
                                   HS[:, k * 512:(k + 1) * 512],
                                   HS[:, k * 512:(k + 1) * 512]), dinc="SQC")
                lb.op("vector", lambda v, k=k:
                      v.tensor_scalar(HV[:, k * 512:(k + 1) * 512],
                                      HS[:, k * 512:(k + 1) * 512],
                                      NWC[:, k:k + 1], None, ALU.mult), dinc="HVC")
            lb.wait("tensor", "SQC", lb.cnt["SQC"])
            for tt in range(4):
                for k in range(KD):
                    lb.op("tensor", lambda t, k=k, tt=tt:
                          t.matmul(ps_ss[:, tt:tt + 1],
                                   SQF[:, k * 512 + tt * P:k * 512 + (tt + 1) * P],
                                   onescol[:], start=(k == 0), stop=(k == KD - 1)),
                          inc="SSK" if k == KD - 1 else None)
            lb.wait("vector", "SSK", lb.cnt["SSK"])
            lb.op("vector", lambda v:
                  v.tensor_scalar(rstds[:], ps_ss[:], 1.0 / D, None, ALU.mult),
                  dinc="RSV")
            lb.wait("scalar", "RSV", lb.cnt["RSV"])
            lb.op("scalar", lambda s: s.activation(rstds[:], rstds[:], AF.Sqrt,
                                                   bias=epsr[:]), dinc="RSS")
            lb.wait("vector", "RSS", lb.cnt["RSS"])
            lb.op("vector", lambda v: v.reciprocal(rstds[:], rstds[:]), dinc="RSV")
            lb.wait("tensor", "HVC", lb.cnt["HVC"])
            lb.wait("tensor", "RSV", lb.cnt["RSV"])
            for tt in range(4):
                for db in range(2):
                    if l2ct >= 2:
                        lb.wait("tensor", "WOEV", l2ct - 1)
                    for k in range(KD):
                        lb.op("tensor", lambda t, k=k, tt=tt, db=db:
                              t.matmul(ps_l2[db][:],
                                       HV[:, k * 512 + tt * P:k * 512 + (tt + 1) * P],
                                       WoutTb[:, k * D + db * 512:k * D + (db + 1) * 512],
                                       start=(k == 0), stop=(k == KD - 1)),
                              inc="WOMM" if k == KD - 1 else None)
                    l2ct += 1
                lb.wait("vector", "WOMM", lb.cnt["WOMM"])
                lb.op("vector", lambda v, tt=tt:
                      v.tensor_scalar(RES[:, 0:512], ps_l2[0][:],
                                      rstds[:, tt:tt + 1], None, ALU.mult))
                lb.op("vector", lambda v, tt=tt:
                      v.tensor_scalar(RES[:, 512:D], ps_l2[1][:],
                                      rstds[:, tt:tt + 1], None, ALU.mult), dinc="WOEV")
                lb.op("vector", lambda v: v.nop(), inc="WOEV")
                lb.wait("sync", "X3V", lb.cnt["X3V"])
                lb.op("sync", lambda s, g=g, tt=tt:
                      s.dma_start(X2T[:], x2[g * 512 + tt * P:g * 512 + (tt + 1) * P, :]),
                      inc="X2LD", dma=True)
                emit_ln(lb, RES[:], NRM[:, 0:D], wk, epsc[:], D)
                lb.wait("vector", "X2LD", lb.cnt["X2LD"])
                lb.wait("vector", "OST", lb.cnt["OST"])
                lb.op("vector", lambda v: v.tensor_add(X3[:], X2T[:], NRM[:]), dinc="X3V")
                lb.wait("vector", "TRP", lb.cnt["TRP"])
                emit_ln(lb, X3[:], XN3[:], wk, epsc[:], D)
                lb.wait("tensor", "LNV", lb.cnt["LNV"])
                emit_transposes(lb, lambda k, _: XN3[:, k * P:(k + 1) * P],
                                lambda k, _: XN3T[:, k * P:(k + 1) * P],
                                ident, ps_tr, 1)
                lb.wait("tensor", "TRE", lb.cnt["TRE"])
                if g > 0 or tt > 0:
                    lb.wait("tensor", "S1EV", lb.cnt["S1EV"])
                for k in range(KD):
                    lb.op("tensor", lambda t, k=k:
                          t.matmul(ps_s1[:], Winb[:, k * FM:(k + 1) * FM],
                                   XN3T[:, k * P:(k + 1) * P],
                                   start=(k == 0), stop=(k == KD - 1)),
                          inc="S1MM" if k == KD - 1 else None)
                lb.wait("vector", "S1MM", lb.cnt["S1MM"])
                lb.wait("vector", "H1MM", lb.cnt["H1MM"])
                lb.op("vector", lambda v:
                      v.tensor_scalar(sTb[:], ps_s1[:], binc[:, 0:1], None, ALU.add),
                      dinc="S1EV")
                lb.wait("tensor", "S1EV", lb.cnt["S1EV"])
                for j in range(KD):
                    pb = j % 2
                    if j >= 2:
                        lb.wait("tensor", "H1EV", lb.cnt["H1MM"] - 1)
                    lb.op("tensor", lambda t, j=j, pb=pb:
                          t.matmul(ps_h1[pb][:], Ew1b[:, j * P:(j + 1) * P],
                                   sTb[:]), inc="H1MM")
                    lb.wait("scalar", "H1MM", lb.cnt["H1MM"])
                    lb.op("scalar", lambda s, j=j, pb=pb:
                          s.activation(hTb[:, j * P:(j + 1) * P], ps_h1[pb][:],
                                       AF.Relu, bias=B1F[:, j:j + 1]), dinc="H1EV")
                lb.wait("tensor", "H1EV", lb.cnt["H1EV"])
                for j in range(KD):
                    lb.op("tensor", lambda t, j=j:
                          t.matmul(ps_s1[:], W2blkb[:, j * FM:(j + 1) * FM],
                                   hTb[:, j * P:(j + 1) * P],
                                   start=(j == 0), stop=(j == KD - 1)),
                          inc="G1MM" if j == KD - 1 else None)
                lb.wait("vector", "G1MM", lb.cnt["G1MM"])
                lb.wait("vector", "WOMM", lb.cnt["WOMM"])
                lb.op("vector", lambda v:
                      v.tensor_scalar(gTb[:], ps_s1[:], b2c[:, 0:1], None, ALU.add),
                      dinc="S1EV")
                lb.wait("tensor", "S1EV", lb.cnt["S1EV"])
                for db in range(2):
                    if l2ct >= 2:
                        lb.wait("tensor", "WOEV", l2ct - 1)
                    lb.op("tensor", lambda t, db=db:
                          t.matmul(ps_l2[db][:], gTb[:],
                                   Wout1b[:, db * 512:(db + 1) * 512],
                                   start=True, stop=False))
                    lb.op("tensor", lambda t, db=db:
                          t.matmul(ps_l2[db][:], ones1b[:],
                                   boutb[:, db * 512:(db + 1) * 512],
                                   start=False, stop=True), inc="WOMM")
                    l2ct += 1
                lb.wait("vector", "WOMM", lb.cnt["WOMM"])
                lb.op("vector", lambda v: v.tensor_copy(RES[:, 0:512], ps_l2[0][:]))
                lb.op("vector", lambda v: v.tensor_copy(RES[:, 512:D], ps_l2[1][:]),
                      dinc="WOEV")
                lb.op("vector", lambda v: v.nop(), inc="WOEV")
                emit_ln(lb, RES[:], NRM[:, 0:D], wk, epsc[:], D)
                lb.op("vector", lambda v: v.tensor_add(X3[:], X3[:], NRM[:]),
                      dinc="X3V")
                lb.wait("sync", "X3V", lb.cnt["X3V"])
                lb.op("sync", lambda s, g=g, tt=tt:
                      s.dma_start(out[g * 512 + tt * P:g * 512 + (tt + 1) * P, :],
                                  X3[:]), inc="OST", dma=True)
        lb.wait("sync", "OST", lb.cnt["OST"])
        lb.finish()
    return nc



def _shard_tokens(x_full):
    out = []
    for c in range(NCORE):
        b, s = c // 4, c % 4
        out.append(np.ascontiguousarray(x_full[b, s * SPAN:(s + 1) * SPAN, :]))
    return out


def _cc(a):
    return np.ascontiguousarray(a)


def kernel(**inputs):
    out, _ = _forward({k: np.asarray(v) for k, v in inputs.items()})
    return out


def _forward(ii):
    x = ii["L0_attn_x"].astype(np.float32)
    spans = _shard_tokens(x)
    cores = list(range(NCORE))

    ncW = build_launch_w()
    w1 = ii["L0_ffn_w1"].astype(np.float32)
    w2 = ii["L0_ffn_w2"].astype(np.float32)
    rW = run_bass_kernel_spmd(
        ncW, [{"w1": _cc(w1[c]), "w2": _cc(w2[c])} for c in cores], core_ids=cores)
    w1b_all = np.stack([np.asarray(rW.results[c]["w1o"]) for c in cores])
    w2b_all = np.stack([np.asarray(rW.results[c]["w2o"]) for c in cores])

    ncA = build_launch_a()
    inA = [{"x": spans[c], "Wq": ii["L0_attn_W_q"], "Wk": ii["L0_attn_W_k"],
            "Wv": ii["L0_attn_W_v"]} for c in cores]
    rA = run_bass_kernel_spmd(ncA, inA, core_ids=cores)
    qT_full = [np.concatenate([np.asarray(rA.results[4 * b + s]["qT"]) for s in range(4)],
                              axis=1) for b in range(B_)]
    kT_full = [np.concatenate([np.asarray(rA.results[4 * b + s]["kT"]) for s in range(4)],
                              axis=1) for b in range(B_)]
    qm_full = [np.concatenate([np.asarray(rA.results[4 * b + s]["qm"]) for s in range(4)],
                              axis=0) for b in range(B_)]
    km_full = [np.concatenate([np.asarray(rA.results[4 * b + s]["km"]) for s in range(4)],
                              axis=0) for b in range(B_)]
    vm_full = [np.concatenate([np.asarray(rA.results[4 * b + s]["vm"]) for s in range(4)],
                              axis=0) for b in range(B_)]

    ncB = build_launch_b()
    inB = []
    for c in cores:
        b = c // 4
        hs = [4 * (c % 4) + j for j in range(4)]
        inB.append({
            "qTh": np.stack([qT_full[b][64 * h:64 * (h + 1)] for h in hs]),
            "kTh": np.stack([kT_full[b][64 * h:64 * (h + 1)] for h in hs]),
            "qmh": np.stack([_cc(qm_full[b][:, 64 * h:64 * (h + 1)]) for h in hs]),
            "kmh": np.stack([_cc(km_full[b][:, 64 * h:64 * (h + 1)]) for h in hs]),
            "vmh": np.stack([_cc(vm_full[b][:, 64 * h:64 * (h + 1)]) for h in hs]),
        })
    rB = run_bass_kernel_spmd(ncB, inB, core_ids=cores)
    oT_full = [np.empty((D, T_), ml_dtypes.bfloat16) for _ in range(B_)]
    for c in cores:
        b = c // 4
        for j in range(4):
            h = 4 * (c % 4) + j
            oT_full[b][64 * h:64 * (h + 1)] = np.asarray(rB.results[c]["oTh"])[j]

    ncC = build_launch_c()
    b1 = ii["L0_ffn_b1"].astype(np.float32)
    inC = []
    for c in cores:
        b, s = c // 4, c % 4
        inC.append({
            "x": spans[c],
            "oT": _cc(oT_full[b][:, s * SPAN:(s + 1) * SPAN]),
            "Wo": ii["L0_attn_W_o"], "rw": ii["L0_ffn_router_w"],
            "w1b": w1b_all,
            "b1L": _cc(b1.reshape(NE, KF, P).transpose(0, 2, 1)),
            "w2b": w2b_all,
            "b2r": ii["L0_ffn_b2"].reshape(NE, 1, D),
            "gmr": ii["L0_ffn_gamma"].reshape(NE, 1, D),
            "btr": ii["L0_ffn_beta"].reshape(NE, 1, D),
        })
    rC = run_bass_kernel_spmd(ncC, inC, core_ids=cores)
    x2_c = [np.asarray(rC.results[c]["x2"]) for c in cores]
    xn2T_full = np.concatenate(
        [np.concatenate([np.asarray(rC.results[4 * b + s]["xn2T"]) for s in range(4)],
                        axis=1) for b in range(B_)], axis=1)

    ncD = build_launch_d()
    WinT = _cc(ii["L1_attn_W_in"].T.astype(np.float32))
    WigT = _cc(ii["L1_attn_W_ig"].T.astype(np.float32))
    WrgT = _cc(ii["L1_attn_W_rg"].T.astype(np.float32))
    h0v = ii["L1_attn_h0"].reshape(D).astype(np.float32)
    inD = []
    for c in cores:
        ch = slice(P * c, P * (c + 1))
        inD.append({
            "xn2T": xn2T_full,
            "WinT": _cc(WinT[:, ch]), "WigT": _cc(WigT[:, ch]),
            "WrgT": _cc(WrgT[:, ch]),
            "big": _cc(ii["L1_attn_b_ig"][ch].reshape(P, 1).astype(np.float32)),
            "brg": _cc(ii["L1_attn_b_rg"][ch].reshape(P, 1).astype(np.float32)),
            "lam": _cc(ii["L1_attn_lam"][ch].reshape(P, 1).astype(np.float32)),
            "h0": _cc(h0v[ch].reshape(P, 1)),
        })
    rD = run_bass_kernel_spmd(ncD, inD, core_ids=cores)
    hT_full = [np.empty((D, T_), np.float32) for _ in range(B_)]
    for c in cores:
        for b in range(B_):
            hT_full[b][P * c:P * (c + 1)] = np.asarray(rD.results[c]["hout"])[b]

    ncF = build_launch_f()
    W1 = ii["L1_ffn_W1"].astype(np.float32)
    W2 = ii["L1_ffn_W2"].astype(np.float32)
    b1f = ii["L1_ffn_b1"].astype(np.float32).reshape(FM * FI)
    Ew1 = np.zeros((FM, D), np.float32)
    W2blk = np.zeros((D, FM), np.float32)
    for mi in range(FM):
        Ew1[mi, mi * FI:(mi + 1) * FI] = W1[mi]
        W2blk[mi * FI:(mi + 1) * FI, mi] = W2[mi]
    inF = []
    for c in cores:
        b, s = c // 4, c % 4
        inF.append({
            "x2": x2_c[c],
            "hT": _cc(hT_full[b][:, s * SPAN:(s + 1) * SPAN]),
            "nwL": _cc(ii["L1_attn_norm_w"].reshape(KD, P).T.astype(np.float32)),
            "WoutT": _cc(ii["L1_attn_W_out"].T.astype(np.float32)),
            "Win": ii["L1_ffn_Win"].astype(np.float32),
            "binc_d": ii["L1_ffn_bin_"].reshape(FM, 1).astype(np.float32),
            "Ew1": Ew1,
            "b1fL": _cc(b1f.reshape(KD, P).T),
            "W2blk": W2blk,
            "b2c_d": ii["L1_ffn_b2"].reshape(FM, 1).astype(np.float32),
            "Wout1": ii["L1_ffn_Wout"].astype(np.float32),
            "boutr": ii["L1_ffn_bout"].reshape(1, D).astype(np.float32),
        })
    rF = run_bass_kernel_spmd(ncF, inF, core_ids=cores)
    out = np.empty((B_, T_, D), np.float32)
    for c in cores:
        b, s = c // 4, c % 4
        out[b, s * SPAN:(s + 1) * SPAN] = np.asarray(rF.results[c]["out"])
    inter = {"oT_full": oT_full, "x2_c": x2_c, "hT_full": hT_full,
             "qT_full": qT_full, "kT_full": kT_full, "vm_full": vm_full,
             "xn2T_full": xn2T_full}
    return out, inter
